# version 07
# baseline (speedup 1.0000x reference)
import numpy as np
import concourse.bass as bass
import concourse.mybir as mybir
from concourse.bass_utils import run_bass_kernel_spmd

F32 = mybir.dt.float32
BF16 = mybir.dt.bfloat16
U8 = mybir.dt.uint8
AF = mybir.ActivationFunctionType
ALU = mybir.AluOpType
AX = mybir.AxisListType

T = 4096
TO = 2048
D = 1024
NF = 8
NPAIR = 4
NQB = 8
I_FF = 4096
EPS = 1e-5

BIAS_FM_SPECS = [
    ("me_eq_b", 8), ("me_bv", 8), ("me_bo", 8), ("me_c1b", 16),
    ("ce_syn_b", 8), ("ce_bv", 8), ("ce_bo", 8), ("ce_n1g", 8), ("ce_n1b", 8),
    ("ce_f1b", 16), ("ce_f2b", 8), ("ce_n2g", 8), ("ce_n2b", 8), ("ml_b1", 32),
]
BIAS_FM_OFF = {}
_o = 0
for _n, _k in BIAS_FM_SPECS:
    BIAS_FM_OFF[_n] = _o
    _o += _k
BIAS_FM_COLS = _o

ENGINES = ("tensor", "vector", "scalar", "gpsimd", "sync")


class Prog:
    def __init__(self):
        self.steps = {e: [] for e in ENGINES}
        self.counts = {}

    def sem(self, name):
        self.counts[name] = 0
        return name

    def cur(self, name):
        return self.counts[name]

    def op(self, eng, fn, waits=(), inc=None, inc_k=1):
        wl = tuple((s, v) for (s, v) in waits if v > 0)
        if inc is not None:
            self.counts[inc] += inc_k
        self.steps[eng].append((fn, wl, inc, inc_k))

    def emit(self, block, handles):
        def mk(eng_name):
            def _(eng):
                for fn, waits, inc, inc_k in self.steps[eng_name]:
                    for s, v in waits:
                        eng.wait_ge(handles[s], v)
                    ins = fn(eng)
                    if inc is not None:
                        assert ins is not None, f"{eng_name} step with inc={inc} returned None"
                        ins.then_inc(handles[inc], inc_k)
            return _
        block.tensor(mk("tensor"))
        block.vector(mk("vector"))
        block.scalar(mk("scalar"))
        block.gpsimd(mk("gpsimd"))
        block.sync(mk("sync"))


def build_nc(debug=False):
    nc = bass.Bass("TRN2", target_bir_lowering=False)

    dT = {}

    def din(name, shape, dtype=F32):
        dT[name] = nc.dram_tensor(name, list(shape), dtype, kind="ExternalInput").ap()

    din("x_full", (T, D))
    din("x_own", (TO, D))
    din("wq", (D, 512)); din("wk", (D, 512)); din("wv", (D, 512)); din("wp", (512, D))
    din("router_w", (D, 4))
    din("sw_w1", (D, I_FF)); din("sw_w3", (D, I_FF)); din("sw_w2", (I_FF, D))
    din("me_eq_w", (D, D)); din("me_wv", (D, D)); din("me_wo", (D, D))
    din("me_c1w", (D, 2 * D)); din("me_c2w", (2 * D, D))
    din("ce_syn_w", (D, D)); din("ce_wv", (D, D)); din("ce_wo", (D, D))
    din("ce_f1w", (D, 2 * D)); din("ce_f2w", (2 * D, D)); din("ce_gen_w", (D, D))
    din("ml_w1", (D, I_FF)); din("ml_w2", (I_FF, D))
    din("bias_fm", (128, BIAS_FM_COLS))
    din("bias_rows_bf", (1, 3 * D))
    din("bias_rows_f32", (1, D + 8))

    out = nc.dram_tensor("out", [TO, D], F32, kind="ExternalOutput").ap()
    rs_in = nc.dram_tensor("rs_in", [T, D], F32, kind="Internal").ap()
    rs_out = nc.dram_tensor("rs_out", [TO, D], F32, kind="Internal").ap()

    dbg = {}
    if debug:
        for nm, shape in [("dbg_xn", (128, NF * T)), ("dbg_hout", (128, NPAIR * T)),
                          ("dbg_w4", (128, 64)), ("dbg_acc", (128, 16 * D)),
                          ("dbg_pp", (T, D)), ("dbg_xnown", (128, NF * TO)),
                          ("dbg_lg", (128, 64))]:
            dbg[nm] = nc.dram_tensor(nm, list(shape), F32, kind="ExternalOutput").ap()

    ARENA_KB = 206
    nc.alloc_sbuf_tensor("arena", [128, ARENA_KB * 1024], U8)
    base = None
    for alloc in nc.m.functions[0].allocations:
        if hasattr(alloc, "memorylocations"):
            for ml in alloc.memorylocations:
                if ml.name == "arena":
                    base = ml.addr
    assert base is not None
    LIMIT = base + ARENA_KB * 1024
    cur = [base]

    def sb(name, shape, dtype, at=None):
        a = cur[0] if at is None else at
        a = (a + 31) // 32 * 32
        t = nc.alloc_sbuf_tensor_at(name, list(shape), dtype, offset=a)
        lo, hi = t.manual_sbuf_range
        assert hi <= LIMIT, f"{name}: {hi} > {LIMIT}"
        if at is None:
            cur[0] = hi
        return t

    R1 = cur[0]
    xn_fm = sb("xn_fm", [128, NF * T], BF16, at=R1)
    out_acc = sb("out_acc", [128, 16 * D], F32, at=R1)
    cur[0] = R1 + 65536
    R2 = cur[0]
    vt = sb("vt", [128, 8 * 32 * 65], BF16, at=R2)
    xn_own = sb("xn_own", [128, NF * TO], BF16, at=R2)
    cur[0] = R2 + 33408
    R3 = cur[0]
    headout = sb("headout", [128, NPAIR * T], BF16, at=R3)
    chA = sb("chA", [128, 4096], BF16, at=R3)
    chB = sb("chB", [128, 4096], BF16, at=R3 + 8192)
    chC = sb("chC", [128, 4096], BF16, at=R3 + 16384)
    chD = sb("chD", [128, 4096], BF16, at=R3 + 24576)
    g_e0 = sb("g_e0", [128, 4 * TO], BF16, at=R3)
    cur[0] = R3 + 32768
    R4 = cur[0]
    k_sb = sb("k_sb", [128, T], BF16, at=R4)
    q_sb = sb("q_sb", [128, T], BF16, at=R4 + 8192)
    xt0 = sb("xt0", [128, D], F32, at=R4)
    xt1 = sb("xt1", [128, D], F32, at=R4 + 4096)
    cen_sb = sb("cen_sb", [128, D], F32, at=R4 + 8192)
    sq_sb = sb("sq_sb", [128, D], BF16, at=R4 + 12288)
    fin_rs = [sb(f"fin_rs{i}", [128, D], F32, at=R3 + 4096 * i) for i in range(4)]
    fin_x = [sb(f"fin_x{i}", [128, D], F32, at=R3 + 16384 + 4096 * i) for i in range(4)]
    cur[0] = R4 + 16384
    R5 = cur[0]
    pt0 = sb("pt0", [128, 1024], BF16, at=R5)
    pt1 = sb("pt1", [128, 1024], BF16, at=R5 + 2048)
    cur[0] = R5 + 4096
    R6 = cur[0]
    wslot = [sb(f"wslot{i}", [128, 4096], BF16, at=R6 + 8192 * i) for i in range(3)]
    pt2 = sb("pt2", [128, 1024], BF16, at=R6 + 16384)
    pt3 = sb("pt3", [128, 1024], BF16, at=R6 + 16384 + 2048)
    oxt0 = sb("oxt0", [128, D], F32, at=R6)
    oxt1 = sb("oxt1", [128, D], F32, at=R6 + 4096)
    ocen = sb("ocen", [128, D], F32, at=R6 + 8192)
    osq = sb("osq", [128, D], BF16, at=R6 + 12288)
    cur[0] = R6 + 24576
    ident = sb("ident", [128, 128], F32)
    iota_f = sb("iota_f", [128, 512], F32, at=R4)
    iota_p = sb("iota_p", [128, 512], F32, at=R4 + 2048)
    masks = sb("masks", [128, 4 * 512], BF16)
    ones_r128_f = sb("ones_r128_f", [1, 128], F32)
    ones_r128_b = sb("ones_r128_b", [1, 128], BF16)
    ones_col_bf = sb("ones_col_bf", [128, 1], BF16)
    bias_fm_sb = sb("bias_fm_sb", [128, BIAS_FM_COLS], F32)
    brow_bf = sb("brow_bf", [1, 3 * D], BF16)
    brow_f = sb("brow_f", [1, D + 8], F32)
    bp_bc = sb("bp_bc", [128, D], F32)
    rbc = sb("rbc", [128, 4], F32)
    w4 = sb("w4", [128, 64], F32)
    negC = sb("negC", [128, 1], F32)
    cm_sb = sb("cm_sb", [1, 8], F32)
    stat_sb = sb("stat_sb", [1, 1024], F32)
    bc_sb = sb("bc_sb", [128, 512], F32)
    vfm_tmp = sb("vfm_tmp", [128, 512], F32)
    asil = sb("asil", [128, 512], F32)
    r_tiny = sb("r_tiny", [128, 8], F32)
    pp0 = sb("pp0", [128, D], F32, at=R4)
    pp1 = sb("pp1", [128, D], F32, at=R4 + 4096)
    eps_t = sb("eps_t", [1, 1], F32)
    epsP = sb("epsP", [128, 1], F32)
    dbg_lg_sb = sb("dbg_lg_sb", [128, 64], F32)
    bnst = sb("bnst", [128, 32], F32)
    xn32 = sb("xn32", [128, D], F32, at=R5)
    rw32 = sb("rw32", [128, 32], F32)
    assert cur[0] <= LIMIT, f"arena overflow: {cur[0] - base} > {ARENA_KB * 1024}"

    ps_st0 = nc.alloc_psum_tensor("ps_st0", [128, 1024], F32)
    ps_st1 = nc.alloc_psum_tensor("ps_st1", [128, 1024], F32)
    ps_oa = nc.alloc_psum_tensor("ps_oa", [128, 512], F32)
    ps_ob = nc.alloc_psum_tensor("ps_ob", [128, 512], F32)
    ps_bc = nc.alloc_psum_tensor("ps_bc", [128, 512], F32)
    ps_x = nc.alloc_psum_tensor("ps_x", [128, 512], F32)

    P = Prog()
    s_dx = P.sem("s_dx"); s_dw = P.sem("s_dw"); s_do = P.sem("s_do")
    s_pe = P.sem("s_pe"); s_act = P.sem("s_act"); s_dve = P.sem("s_dve")
    s_qk = P.sem("s_qk"); s_exp = P.sem("s_exp"); s_msk = P.sem("s_msk")
    s_av = P.sem("s_av"); s_cc = P.sem("s_cc"); s_po = P.sem("s_po")
    s_xa = P.sem("s_xa"); s_xb = P.sem("s_xb")
    s_xc = P.sem("s_xc"); s_xd = P.sem("s_xd")

    def bias_col(name, chunk):
        o = BIAS_FM_OFF[name] + chunk
        return bias_fm_sb[:, o:o + 1]

    P.op("gpsimd", lambda g: g.iota(iota_f[:], pattern=[[1, 512]], base=0, channel_multiplier=0,
                                    allow_small_or_imprecise_dtypes=True), inc=s_po)
    P.op("gpsimd", lambda g: g.iota(iota_p[:], pattern=[[0, 512]], base=0, channel_multiplier=1,
                                    allow_small_or_imprecise_dtypes=True), inc=s_po)
    P.op("gpsimd", lambda g: g.memset(vt[:], 1.0), inc=s_po)
    P.op("gpsimd", lambda g: g.memset(ones_r128_f[:], 1.0), inc=s_po)
    P.op("gpsimd", lambda g: g.memset(ones_r128_b[:], 1.0), inc=s_po)
    P.op("gpsimd", lambda g: g.memset(ones_col_bf[:], 1.0), inc=s_po)
    P.op("gpsimd", lambda g: g.memset(eps_t[:], EPS), inc=s_po)
    P.op("gpsimd", lambda g: g.memset(epsP[:], EPS), inc=s_po)
    P.op("gpsimd", lambda g: g.dma_start(out=bias_fm_sb[:], in_=dT["bias_fm"]), inc=s_dw, inc_k=16)
    P.op("gpsimd", lambda g: g.dma_start(out=brow_bf[:], in_=dT["bias_rows_bf"]), inc=s_dw, inc_k=16)
    P.op("gpsimd", lambda g: g.dma_start(out=brow_f[:], in_=dT["bias_rows_f32"]), inc=s_dw, inc_k=16)
    brows_ld = P.cur(s_dw)
    P.op("vector", lambda v: v.tensor_tensor(ident[:], iota_f[:, 0:128], iota_p[:, 0:128],
                                             ALU.is_equal), waits=[(s_po, 2)], inc=s_dve)
    for mi in range(4):
        P.op("vector", lambda v, mi=mi: v.tensor_scalar(bc_sb[:], iota_f[:], 128.0 * mi, None,
                                                        ALU.subtract),
             waits=[(s_po, 2), (s_dve, P.cur(s_dve))], inc=s_dve)
        P.op("vector", lambda v, mi=mi: v.tensor_tensor(masks[:, 512 * mi:512 * (mi + 1)],
                                                        bc_sb[:], iota_p[:], ALU.is_ge),
             waits=[(s_dve, P.cur(s_dve))], inc=s_dve)
    def bpbc_mm(t):
        t.matmul(ps_bc[:, 0:512], ones_r128_f[0:1, :], brow_f[0:1, 0:512], start=True, stop=True)
        return t.matmul(ps_x[:, 0:512], ones_r128_f[0:1, :], brow_f[0:1, 512:1024], start=True, stop=True)
    P.op("tensor", bpbc_mm, waits=[(s_dw, brows_ld), (s_po, 8)], inc=s_pe)
    P.op("vector", lambda v: v.tensor_copy(bp_bc[:, 0:512], ps_bc[:]),
         waits=[(s_pe, P.cur(s_pe))], inc=s_dve)
    P.op("vector", lambda v: v.tensor_copy(bp_bc[:, 512:1024], ps_x[:]), inc=s_dve)
    bp_done = P.cur(s_dve)
    P.op("tensor", lambda t: t.matmul(ps_bc[:, 0:8], ones_r128_f[0:1, :], brow_f[0:1, 1024:1032],
                                      start=True, stop=True), waits=[(s_dve, bp_done)], inc=s_pe)
    P.op("vector", lambda v: v.tensor_copy(rbc[:], ps_bc[:, 0:4]),
         waits=[(s_pe, P.cur(s_pe))], inc=s_dve)
    P.op("vector", lambda v: v.tensor_tensor(rbc[:], rbc[:], ps_bc[:, 4:8], ALU.add),
         waits=[(s_dve, P.cur(s_dve))], inc=s_dve)
    setup_dve = P.cur(s_dve)
    setup_pe = P.cur(s_pe)

    ln_done = {}
    xp_evac = {}
    for tt in range(32):
        xt = xt0 if tt % 2 == 0 else xt1
        w = [(s_dve, ln_done[tt - 2])] if tt >= 2 else [(s_dve, setup_dve), (s_pe, setup_pe)]
        P.op("sync", lambda sy, tt=tt, xt=xt: sy.dma_start(
            out=xt[:], in_=dT["x_full"][128 * tt:128 * (tt + 1), :]), waits=w, inc=s_dx, inc_k=16)
        q0 = 16 * (tt % 2)
        P.op("vector", lambda v, xt=xt, q0=q0: v.bn_stats(bnst[:, q0:q0 + 6], xt[:, 0:512]),
             waits=[(s_dx, P.cur(s_dx))], inc=s_dve)
        P.op("vector", lambda v, xt=xt, q0=q0: v.bn_stats(bnst[:, q0 + 6:q0 + 12], xt[:, 512:1024]),
             inc=s_dve)
        P.op("vector", lambda v, q0=q0: v.bn_aggr(bnst[:, q0 + 12:q0 + 14], bnst[:, q0:q0 + 12]),
             waits=[(s_dve, P.cur(s_dve))], inc=s_dve)
        bna = P.cur(s_dve)
        P.op("scalar", lambda s, q0=q0: s.activation(bnst[:, q0 + 14:q0 + 15], bnst[:, q0 + 13:q0 + 14],
                                                     AF.Ln, bias=epsP[:]),
             waits=[(s_dve, bna), (s_po, 8)], inc=s_act)
        P.op("scalar", lambda s, q0=q0: s.activation(bnst[:, q0 + 15:q0 + 16], bnst[:, q0 + 14:q0 + 15],
                                                     AF.Exp, scale=-0.5),
             waits=[(s_act, P.cur(s_act))], inc=s_act)
        P.op("vector", lambda v, xt=xt, q0=q0: v.tensor_scalar(cen_sb[:], xt[:], bnst[:, q0 + 12:q0 + 13],
                                                               bnst[:, q0 + 15:q0 + 16],
                                                               ALU.subtract, ALU.mult),
             waits=[(s_act, P.cur(s_act))], inc=s_dve)
        scaled = P.cur(s_dve)
        for f in range(NF):
            gi = tt * NF + f
            bank = (ps_oa, ps_ob, ps_bc)[gi % 3]
            w = [(s_dve, scaled)]
            if gi >= 3:
                w.append((s_dve, xp_evac[gi - 3]))
            elif gi == 0:
                w.append((s_dve, setup_dve))
                w.append((s_pe, setup_pe))
            P.op("tensor", lambda t, f=f, bank=bank: t.transpose(
                bank[:, 0:128], cen_sb[:, 128 * f:128 * (f + 1)], ident[:]), waits=w, inc=s_pe)
            P.op("vector", lambda v, f=f, tt=tt, bank=bank: v.tensor_copy(
                xn_fm[:, T * f + 128 * tt:T * f + 128 * (tt + 1)], bank[:, 0:128]),
                 waits=[(s_pe, P.cur(s_pe))], inc=s_dve)
            xp_evac[gi] = P.cur(s_dve)
        ln_done[tt] = P.cur(s_dve)
    p1_dve = P.cur(s_dve)

    wv_ld = {}
    vproj_done = {}
    vt_evac = {}
    vfm_evac = {}

    def emit_vload(p):
        slot = wslot[p % 3]
        w = [] if p < 3 else [(s_pe, vproj_done[p - 3])]
        P.op("gpsimd", lambda g, p=p, slot=slot: g.dma_start(
            out=slot[:, 0:1024].rearrange("p (a j) -> p a j", a=8),
            in_=dT["wv"][:, 128 * p:128 * (p + 1)].rearrange("(a p) j -> p a j", p=128)),
             waits=w, inc=s_dw, inc_k=16)
        wv_ld[p] = P.cur(s_dw)

    emit_vload(0)
    emit_vload(1)
    for p in range(NPAIR):
        slot = wslot[p % 3]
        hA, hB = 2 * p, 2 * p + 1
        for tt in range(8):
            it = p * 8 + tt
            bank = (ps_st0, ps_st1)[tt % 2]
            w = [(s_dw, wv_ld[p]), (s_dve, p1_dve)]
            if it >= 2:
                w.append((s_act, vfm_evac[it - 2]))
            def vmm(t, slot=slot, tt=tt, bank=bank):
                last = None
                for f in range(NF):
                    last = t.matmul(bank[:, 0:512], slot[:, 128 * f:128 * (f + 1)],
                                    xn_fm[:, T * f + 512 * tt:T * f + 512 * (tt + 1)],
                                    start=(f == 0), stop=(f == NF - 1))
                return last
            P.op("tensor", vmm, waits=w, inc=s_pe)
            mmn = P.cur(s_pe)
            P.op("scalar", lambda s, bank=bank: s.copy(vfm_tmp[:], bank[:, 0:512]),
                 waits=[(s_pe, mmn)], inc=s_act)
            vfm_evac[it] = P.cur(s_act)
            for j4 in range(4):
                kt = tt * 4 + j4
                gk = p * 32 + kt
                tb = (ps_oa, ps_ob, ps_bc)[gk % 3]
                w = [(s_act, vfm_evac[it])]
                if gk >= 3:
                    w.append((s_dve, vt_evac[gk - 3]))
                P.op("tensor", lambda t, j4=j4, tb=tb: t.transpose(
                    tb[:, 0:128], vfm_tmp[:, 128 * j4:128 * (j4 + 1)], ident[:]), waits=w, inc=s_pe)
                tpn = P.cur(s_pe)
                def evv(v, kt=kt, tb=tb, hA=hA, hB=hB):
                    v.tensor_copy(vt[:, 65 * (32 * hA + kt):65 * (32 * hA + kt) + 64], tb[:, 0:64])
                    return v.tensor_copy(vt[:, 65 * (32 * hB + kt):65 * (32 * hB + kt) + 64],
                                         tb[:, 64:128])
                P.op("vector", evv, waits=[(s_pe, tpn), (s_po, 8)], inc=s_dve)
                vt_evac[gk] = P.cur(s_dve)
        vproj_done[p] = P.cur(s_pe)
        if p + 2 < NPAIR:
            emit_vload(p + 2)

    unit = 0
    exp_at = {}
    av_at = {}
    kq_proj_done = {}
    oa_free = {}
    raw_used = {}
    bc_busy = {}
    pending_norm = None
    for p in range(NPAIR):
        hA, hB = 2 * p, 2 * p + 1
        sK, sQ = wslot[0], wslot[1]
        wld = []
        if p == 0:
            wld = [(s_pe, vproj_done[min(3, NPAIR - 1)])]
        else:
            wld = [(s_pe, kq_proj_done[p - 1])]
        P.op("gpsimd", lambda g, p=p, sK=sK: g.dma_start(
            out=sK[:, 0:1024].rearrange("p (a j) -> p a j", a=8),
            in_=dT["wk"][:, 128 * p:128 * (p + 1)].rearrange("(a p) j -> p a j", p=128)),
             waits=wld, inc=s_dw, inc_k=16)
        P.op("gpsimd", lambda g, p=p, sQ=sQ: g.dma_start(
            out=sQ[:, 0:1024].rearrange("p (a j) -> p a j", a=8),
            in_=dT["wq"][:, 128 * p:128 * (p + 1)].rearrange("(a p) j -> p a j", p=128)),
             waits=wld, inc=s_dw, inc_k=16)
        kq_w = P.cur(s_dw)
        for which, wsl, dst in (("k", sK, k_sb), ("q", sQ, q_sb)):
            for tt in range(8):
                bank = (ps_st0, ps_st1)[tt % 2]
                w = [(s_dw, kq_w), (s_act, P.cur(s_act))]
                if p == 0 and which == "k" and tt < 2:
                    w.append((s_dve, P.cur(s_dve)))
                if p > 0 and which == "k" and tt == 0:
                    w.append((s_av, P.cur(s_av)))
                def kqm(t, wsl=wsl, tt=tt, bank=bank):
                    last = None
                    for f in range(NF):
                        last = t.matmul(bank[:, 0:512], wsl[:, 128 * f:128 * (f + 1)],
                                        xn_fm[:, T * f + 512 * tt:T * f + 512 * (tt + 1)],
                                        start=(f == 0), stop=(f == NF - 1))
                    return last
                P.op("tensor", kqm, waits=w, inc=s_pe)
                mmn = P.cur(s_pe)
                P.op("scalar", lambda s, dst=dst, tt=tt, bank=bank, which=which: s.activation(
                    dst[:, 512 * tt:512 * (tt + 1)], bank[:, 0:512], AF.Identity,
                    scale=(0.125 if which == "q" else 1.0)), waits=[(s_pe, mmn)], inc=s_act)
        kq_act = P.cur(s_act)
        kq_proj_done[p] = P.cur(s_pe)
        P.op("vector", lambda v: v.tensor_reduce(r_tiny[:, 5:6], k_sb[:], AX.X, ALU.max,
                                                 apply_absolute_value=True),
             waits=[(s_act, kq_act)], inc=s_dve)
        P.op("vector", lambda v: v.tensor_reduce(r_tiny[:, 6:7], q_sb[:], AX.X, ALU.max,
                                                 apply_absolute_value=True), inc=s_dve)
        mx = P.cur(s_dve)
        P.op("tensor", lambda t: t.transpose(ps_x[0:1, 0:128], r_tiny[:, 5:6], ident[:]),
             waits=[(s_dve, mx)], inc=s_pe)
        P.op("tensor", lambda t: t.transpose(ps_x[0:1, 128:256], r_tiny[:, 6:7], ident[:]),
             inc=s_pe)
        tpd = P.cur(s_pe)
        P.op("vector", lambda v: v.tensor_reduce(cm_sb[0:1, 0:1], ps_x[0:1, 0:128], AX.X, ALU.max),
             waits=[(s_pe, tpd)], inc=s_dve)
        P.op("vector", lambda v: v.tensor_reduce(cm_sb[0:1, 1:2], ps_x[0:1, 128:256], AX.X, ALU.max),
             inc=s_dve)
        P.op("vector", lambda v: v.tensor_tensor(cm_sb[0:1, 2:3], cm_sb[0:1, 0:1], cm_sb[0:1, 1:2],
                                                 ALU.mult), waits=[(s_dve, P.cur(s_dve))], inc=s_dve)
        P.op("vector", lambda v: v.tensor_scalar(cm_sb[0:1, 3:4], cm_sb[0:1, 2:3], -64.0, 40.0,
                                                 ALU.mult, ALU.add),
             waits=[(s_dve, P.cur(s_dve))], inc=s_dve)
        P.op("tensor", lambda t: t.matmul(ps_x[:, 256:257], ones_r128_f[0:1, :], cm_sb[0:1, 3:4],
                                          start=True, stop=True),
             waits=[(s_dve, P.cur(s_dve))], inc=s_pe)
        P.op("vector", lambda v: v.tensor_copy(negC[:], ps_x[:, 256:257]),
             waits=[(s_pe, P.cur(s_pe))], inc=s_dve)
        negc_done = P.cur(s_dve)

        pending_norm = pending_norm if p > 0 else None
        for qb in range(NQB):
            ncq = 4 * (qb + 1)
            qb_prev = (qb - 1) if qb > 0 else (NQB - 1 if p > 0 else None)
            pending_av = None
            for kc in range(ncq):
                if kc == 4 and pending_norm is not None:
                    pending_norm()
                    pending_norm = None
                stb = (ps_st0, ps_st1)[unit % 2]
                ptb = (pt0, pt1, pt2, pt3)[unit % 4]
                band = kc >= ncq - 4
                w = [(s_dve, negc_done)]
                if unit >= 2:
                    w.append((s_exp, exp_at[unit - 2]))
                def qk(t, kc=kc, qb=qb, stb=stb):
                    t.matmul(stb[:, 0:512], k_sb[0:64, 128 * kc:128 * (kc + 1)],
                             q_sb[0:64, 512 * qb:512 * (qb + 1)], start=True, stop=True,
                             tile_position=(0, 0))
                    return t.matmul(stb[:, 512:1024], k_sb[64:128, 128 * kc:128 * (kc + 1)],
                                    q_sb[64:128, 512 * qb:512 * (qb + 1)], start=True, stop=True,
                                    tile_position=(64, 0))
                P.op("tensor", qk, waits=w, inc=s_qk)
                w = [(s_qk, P.cur(s_qk))]
                if unit >= 4:
                    w.append((s_av, av_at[unit - 4]))
                P.op("scalar", lambda s, stb=stb, ptb=ptb: s.activation(ptb[:], stb[:], AF.Exp,
                                                                       bias=negC[:]),
                     waits=w, inc=s_exp)
                exp_at[unit] = P.cur(s_exp)
                if band:
                    mi = kc - (ncq - 4)
                    def msk(v, ptb=ptb, mi=mi):
                        v.tensor_tensor(ptb[:, 0:512], ptb[:, 0:512],
                                        masks[:, 512 * mi:512 * (mi + 1)], ALU.mult)
                        return v.tensor_tensor(ptb[:, 512:1024], ptb[:, 512:1024],
                                               masks[:, 512 * mi:512 * (mi + 1)], ALU.mult)
                    P.op("vector", msk, waits=[(s_exp, exp_at[unit])], inc=s_msk)
                w = [(s_exp, exp_at[unit])]
                if band:
                    w.append((s_msk, P.cur(s_msk)))
                if kc == 0 and qb_prev is not None:
                    w.append((s_act, oa_free[qb_prev]))
                def av(t, kc=kc, ptb=ptb, hA=hA, hB=hB, first=(kc == 0), last=(kc == ncq - 1)):
                    t.matmul(ps_oa[0:65, :], vt[:, 65 * (32 * hA + kc):65 * (32 * hA + kc) + 65],
                             ptb[:, 0:512], start=first, stop=last)
                    return t.matmul(ps_ob[0:65, :], vt[:, 65 * (32 * hB + kc):65 * (32 * hB + kc) + 65],
                                    ptb[:, 512:1024], start=first, stop=last)
                if pending_av is not None:
                    fn_av, w_av, u_av = pending_av
                    P.op("tensor", fn_av, waits=w_av, inc=s_av)
                    av_at[u_av] = P.cur(s_av)
                pending_av = (av, w, unit)
                unit += 1
            if pending_av is not None:
                fn_av, w_av, u_av = pending_av
                P.op("tensor", fn_av, waits=w_av, inc=s_av)
                av_at[u_av] = P.cur(s_av)
                pending_av = None
            if pending_norm is not None:
                pending_norm()
                pending_norm = None
            w_raw = [(s_av, P.cur(s_av))]
            if qb_prev is not None:
                w_raw.append((s_dve, raw_used[qb_prev]))
            P.op("scalar", lambda s: s.copy(vfm_tmp[0:65, :], ps_oa[0:65, :]),
                 waits=w_raw, inc=s_act)
            P.op("scalar", lambda s: s.copy(asil[0:65, :], ps_ob[0:65, :]),
                 waits=[(s_act, P.cur(s_act))], inc=s_act)
            raw_ev = P.cur(s_act)
            oa_free[qb] = raw_ev
            P.op("vector", lambda v: v.reciprocal(stat_sb[0:1, 0:512], vfm_tmp[64:65, :]),
                 waits=[(s_act, raw_ev)], inc=s_dve)
            P.op("vector", lambda v: v.reciprocal(stat_sb[0:1, 512:1024], asil[64:65, :]),
                 waits=[(s_dve, P.cur(s_dve))], inc=s_dve)
            rec = P.cur(s_dve)

            def mk_norm(p=p, qb=qb, rec=rec, wprev=(bc_busy.get(qb_prev) if qb_prev is not None else None)):
                def emit_norm():
                    wbc = [(s_dve, rec)]
                    if wprev is not None:
                        wbc.append((s_dve, wprev))
                    P.op("tensor", lambda t: t.matmul(ps_bc[0:64, :], ones_r128_f[0:1, 0:64],
                                                      stat_sb[0:1, 0:512], start=True, stop=True),
                         waits=wbc, inc=s_pe)
                    P.op("tensor", lambda t: t.matmul(ps_bc[64:128, :], ones_r128_f[0:1, 0:64],
                                                      stat_sb[0:1, 512:1024], start=True, stop=True,
                                                      tile_position=(0, 64)), inc=s_pe)
                    bcd = P.cur(s_pe)
                    def norm(v, p=p, qb=qb):
                        v.tensor_tensor(headout[0:64, T * p + 512 * qb:T * p + 512 * (qb + 1)],
                                        vfm_tmp[0:64, :], ps_bc[0:64, :], ALU.mult)
                        return v.tensor_tensor(headout[64:128, T * p + 512 * qb:T * p + 512 * (qb + 1)],
                                               asil[0:64, :], ps_bc[64:128, :], ALU.mult)
                    P.op("vector", norm, waits=[(s_pe, bcd)], inc=s_dve)
                    raw_used[qb] = P.cur(s_dve)
                    bc_busy[qb] = P.cur(s_dve)
                return emit_norm
            pending_norm = mk_norm()
    if pending_norm is not None:
        pending_norm()
        pending_norm = None
    attn_dve = P.cur(s_dve)
    attn_av = P.cur(s_av)
    attn_pe = P.cur(s_pe)

    if debug:
        P.op("gpsimd", lambda g: g.dma_start(out=dbg["dbg_xn"], in_=xn_fm[:]),
             waits=[(s_dve, p1_dve)], inc=s_dw, inc_k=16)
        P.op("gpsimd", lambda g: g.dma_start(out=dbg["dbg_hout"], in_=headout[:]),
             waits=[(s_dve, attn_dve)], inc=s_dw, inc_k=16)

    P.op("gpsimd", lambda g: g.dma_start(
        out=wslot[2][:, 0:4096].rearrange("p (a n) -> p a n", a=4),
        in_=dT["wp"].rearrange("(a p) n -> p a n", p=128)),
         waits=[(s_av, attn_av)], inc=s_dw, inc_k=16)
    wp_ld = P.cur(s_dw)
    pp_ev = {}
    st_cnt = {}
    for tt in range(32):
        ppb = pp0 if tt % 2 == 0 else pp1
        for half in range(2):
            hb = (ps_oa, ps_ob)[half]
            w = [(s_dw, wp_ld), (s_dve, attn_dve)]
            if tt >= 1:
                w.append((s_dve, pp_ev[(tt - 1, half)]))
            def ppm(t, tt=tt, half=half, hb=hb):
                last = None
                for c in range(4):
                    last = t.matmul(hb[:, 0:512],
                                    headout[:, T * c + 128 * tt:T * c + 128 * (tt + 1)],
                                    wslot[2][:, 1024 * c + 512 * half:1024 * c + 512 * (half + 1)],
                                    start=(c == 0), stop=(c == 3))
                return last
            P.op("tensor", ppm, waits=w, inc=s_pe)
            mmn = P.cur(s_pe)
            w = [(s_pe, mmn)]
            if tt >= 2:
                w.append((s_dx, st_cnt[tt - 2]))
            P.op("vector", lambda v, ppb=ppb, half=half, hb=hb: v.tensor_copy(
                ppb[:, 512 * half:512 * (half + 1)], hb[:, 0:512]), waits=w, inc=s_dve)
            pp_ev[(tt, half)] = P.cur(s_dve)
        P.op("sync", lambda sy, tt=tt, ppb=ppb: sy.dma_start(
            out=rs_in[128 * tt:128 * (tt + 1), :], in_=ppb[:]),
             waits=[(s_dve, pp_ev[(tt, 1)])], inc=s_dx, inc_k=16)
        st_cnt[tt] = P.cur(s_dx)
    rs_ready = P.cur(s_dx)
    P.op("gpsimd", lambda g: g.collective_compute(
        "ReduceScatter", ALU.add, replica_groups=[[0, 1], [2, 3], [4, 5], [6, 7]],
        ins=[rs_in.opt()], outs=[rs_out.opt()]), waits=[(s_dx, rs_ready)], inc=s_cc)
    if debug:
        P.op("gpsimd", lambda g: g.dma_start(out=dbg["dbg_pp"], in_=rs_in),
             waits=[(s_dx, rs_ready)], inc=s_dw, inc_k=16)

    P.op("gpsimd", lambda g: g.dma_start(
        out=rw32[:, 0:32].rearrange("p (a e) -> p a e", a=8),
        in_=dT["router_w"].rearrange("(a p) e -> p a e", p=128)),
         waits=[(s_av, attn_av)], inc=s_dw, inc_k=16)
    rw_ld = P.cur(s_dw)
    oln = {}
    oxp = {}
    for tt in range(16):
        xt = oxt0 if tt % 2 == 0 else oxt1
        w = [(s_pe, kq_proj_done[NPAIR - 1])]
        if tt >= 2:
            w.append((s_dve, oln[tt - 2]))
        sx = s_xa if tt % 2 == 0 else s_xb
        P.op("sync", lambda sy, tt=tt, xt=xt: sy.dma_start(
            out=xt[:], in_=dT["x_own"][128 * tt:128 * (tt + 1), :]), waits=w, inc=sx, inc_k=16)
        q0 = 16 * (tt % 2)
        P.op("vector", lambda v, xt=xt, q0=q0: v.bn_stats(bnst[:, q0:q0 + 6], xt[:, 0:512]),
             waits=[(sx, P.cur(sx)), (s_dve, attn_dve)], inc=s_dve)
        P.op("vector", lambda v, xt=xt, q0=q0: v.bn_stats(bnst[:, q0 + 6:q0 + 12], xt[:, 512:1024]),
             inc=s_dve)
        P.op("vector", lambda v, q0=q0: v.bn_aggr(bnst[:, q0 + 12:q0 + 14], bnst[:, q0:q0 + 12]),
             waits=[(s_dve, P.cur(s_dve))], inc=s_dve)
        bna = P.cur(s_dve)
        P.op("scalar", lambda s, q0=q0: s.activation(bnst[:, q0 + 14:q0 + 15], bnst[:, q0 + 13:q0 + 14],
                                                     AF.Ln, bias=epsP[:]),
             waits=[(s_dve, bna)], inc=s_act)
        P.op("scalar", lambda s, q0=q0: s.activation(bnst[:, q0 + 15:q0 + 16], bnst[:, q0 + 14:q0 + 15],
                                                     AF.Exp, scale=-0.5),
             waits=[(s_act, P.cur(s_act))], inc=s_act)
        P.op("vector", lambda v, xt=xt, q0=q0: v.tensor_scalar(ocen[:], xt[:], bnst[:, q0 + 12:q0 + 13],
                                                               bnst[:, q0 + 15:q0 + 16],
                                                               ALU.subtract, ALU.mult),
             waits=[(s_act, P.cur(s_act))], inc=s_dve)
        sc = P.cur(s_dve)
        for f in range(NF):
            gi = tt * NF + f
            bank = (ps_bc, ps_oa)[gi % 2]
            w = [(s_dve, sc)]
            if gi >= 2:
                w.append((s_dve, oxp[gi - 2]))
            P.op("tensor", lambda t, f=f, bank=bank: t.transpose(
                bank[:, 0:128], ocen[:, 128 * f:128 * (f + 1)], ident[:]), waits=w, inc=s_pe)
            tpn = P.cur(s_pe)
            P.op("vector", lambda v, f=f, bank=bank: v.tensor_copy(
                xn32[:, 128 * f:128 * (f + 1)], bank[:, 0:128]),
                 waits=[(s_pe, tpn)], inc=s_dve)
            P.op("vector", lambda v, f=f, tt=tt, bank=bank: v.tensor_copy(
                xn_own[:, TO * f + 128 * tt:TO * f + 128 * (tt + 1)], bank[:, 0:128]),
                 inc=s_dve)
            oxp[gi] = P.cur(s_dve)
        xn32_done = P.cur(s_dve)
        w = [(s_dw, rw_ld), (s_dve, xn32_done)]
        if tt >= 1:
            w.append((s_dve, oln[tt - 1]))
        def rmm(t):
            last = None
            for f in range(NF):
                last = t.matmul(ps_x[:, 0:4], xn32[:, 128 * f:128 * (f + 1)],
                                rw32[:, 4 * f:4 * (f + 1)],
                                start=(f == 0), stop=(f == NF - 1))
            return last
        P.op("tensor", rmm, waits=w, inc=s_pe)
        rm = P.cur(s_pe)
        P.op("vector", lambda v: v.tensor_tensor(r_tiny[:, 0:4], ps_x[:, 0:4], rbc[:], ALU.add),
             waits=[(s_pe, rm)], inc=s_dve)
        if debug:
            P.op("vector", lambda v, tt=tt: v.tensor_copy(dbg_lg_sb[:, 4 * tt:4 * (tt + 1)],
                                                          r_tiny[:, 0:4]),
                 waits=[(s_dve, P.cur(s_dve))], inc=s_dve)
        P.op("vector", lambda v: v.tensor_reduce(r_tiny[:, 4:5], r_tiny[:, 0:4], AX.X, ALU.max),
             waits=[(s_dve, P.cur(s_dve))], inc=s_dve)
        P.op("vector", lambda v: v.tensor_tensor(r_tiny[:, 5:6], r_tiny[:, 0:1], r_tiny[:, 1:2],
                                                 ALU.max), waits=[(s_dve, P.cur(s_dve))], inc=s_dve)
        P.op("vector", lambda v: v.tensor_tensor(r_tiny[:, 6:7], r_tiny[:, 0:1], r_tiny[:, 1:2],
                                                 ALU.min), waits=[(s_dve, P.cur(s_dve))], inc=s_dve)
        P.op("vector", lambda v: v.tensor_tensor(bc_sb[:, 0:1], r_tiny[:, 2:3], r_tiny[:, 3:4],
                                                 ALU.max), waits=[(s_dve, P.cur(s_dve))], inc=s_dve)
        P.op("vector", lambda v: v.tensor_tensor(bc_sb[:, 1:2], r_tiny[:, 2:3], r_tiny[:, 3:4],
                                                 ALU.min), waits=[(s_dve, P.cur(s_dve))], inc=s_dve)
        P.op("vector", lambda v: v.tensor_tensor(bc_sb[:, 2:3], r_tiny[:, 5:6], bc_sb[:, 0:1],
                                                 ALU.min), waits=[(s_dve, P.cur(s_dve))], inc=s_dve)
        P.op("vector", lambda v: v.tensor_tensor(bc_sb[:, 3:4], r_tiny[:, 6:7], bc_sb[:, 1:2],
                                                 ALU.max), waits=[(s_dve, P.cur(s_dve))], inc=s_dve)
        P.op("vector", lambda v: v.tensor_tensor(bc_sb[:, 4:5], bc_sb[:, 2:3], bc_sb[:, 3:4],
                                                 ALU.max), waits=[(s_dve, P.cur(s_dve))], inc=s_dve)
        P.op("vector", lambda v: v.tensor_scalar(r_tiny[:, 7:8], r_tiny[:, 4:5], -1.0, None,
                                                 ALU.mult), waits=[(s_dve, P.cur(s_dve))], inc=s_dve)
        pre_exp = P.cur(s_dve)
        P.op("scalar", lambda s: s.activation(bc_sb[:, 8:12], r_tiny[:, 0:4], AF.Exp,
                                              bias=r_tiny[:, 7:8]),
             waits=[(s_dve, pre_exp)], inc=s_act)
        P.op("vector", lambda v: v.tensor_scalar(bc_sb[:, 12:16], r_tiny[:, 0:4], bc_sb[:, 4:5],
                                                 1.0, ALU.is_ge, ALU.mult),
             waits=[(s_act, P.cur(s_act))], inc=s_dve)
        P.op("vector", lambda v: v.tensor_tensor(bc_sb[:, 16:20], bc_sb[:, 8:12], bc_sb[:, 12:16],
                                                 ALU.mult), waits=[(s_dve, P.cur(s_dve))], inc=s_dve)
        P.op("vector", lambda v: v.tensor_reduce(bc_sb[:, 20:21], bc_sb[:, 16:20], AX.X, ALU.add),
             waits=[(s_dve, P.cur(s_dve))], inc=s_dve)
        P.op("vector", lambda v: v.reciprocal(bc_sb[:, 21:22], bc_sb[:, 20:21]),
             waits=[(s_dve, P.cur(s_dve))], inc=s_dve)
        P.op("vector", lambda v, tt=tt: v.tensor_scalar(w4[:, 4 * tt:4 * (tt + 1)],
                                                        bc_sb[:, 16:20], bc_sb[:, 21:22], 0.0,
                                                        ALU.mult, ALU.add),
             waits=[(s_dve, P.cur(s_dve))], inc=s_dve)
        oln[tt] = P.cur(s_dve)
    xn_own_done = P.cur(s_dve)
    router_done = P.cur(s_dve)
    if debug:
        P.op("gpsimd", lambda g: g.dma_start(out=dbg["dbg_w4"], in_=w4[:]),
             waits=[(s_dve, router_done)], inc=s_dw, inc_k=16)
        P.op("gpsimd", lambda g: g.dma_start(out=dbg["dbg_lg"], in_=dbg_lg_sb[:]),
             waits=[(s_dve, router_done)], inc=s_dw, inc_k=16)
        P.op("gpsimd", lambda g: g.dma_start(out=dbg["dbg_xnown"], in_=xn_own[:]),
             waits=[(s_dve, xn_own_done)], inc=s_dw, inc_k=16)

    slot_free = {wslot[0].name: [(s_dve, xn_own_done)],
                 wslot[1].name: [(s_dve, xn_own_done)],
                 wslot[2].name: [(s_pe, P.cur(s_pe))]}
    wslot3 = sb("wslot3", [128, 4096], BF16, at=R4)
    wslot4 = sb("wslot4", [128, 4096], BF16, at=R4 + 8192)
    ring_slots = [wslot[0], wslot[1], wslot[2], wslot3, wslot4]
    slot_free[wslot3.name] = [(s_dx, P.cur(s_dx))]
    slot_free[wslot4.name] = [(s_av, P.cur(s_av))]
    RING = [0]
    ST_EVAC = {0: None, 1: None}

    def ring_slot():
        s = ring_slots[RING[0] % 5]
        RING[0] += 1
        return s
    eout_guard = {0: 0, 1: 0}

    def load_w_cols(wname, c0, c1, nrows, slot):
        nch = nrows // 128
        ncol = c1 - c0
        P.op("gpsimd", lambda g: g.dma_start(
            out=slot[:, 0:nch * ncol].rearrange("p (a j) -> p a j", a=nch),
            in_=dT[wname][:, c0:c1].rearrange("(a p) j -> p a j", p=128)),
             waits=slot_free[slot.name], inc=s_dw, inc_k=16)
        return P.cur(s_dw)

    def load_w_rows(wname, r0, r1, slot):
        nch = (r1 - r0) // 128
        P.op("gpsimd", lambda g: g.dma_start(
            out=slot[:, 0:nch * 1024].rearrange("p (a n) -> p a n", a=nch),
            in_=dT[wname][r0:r1, :].rearrange("(a p) n -> p a n", p=128)),
             waits=slot_free[slot.name], inc=s_dw, inc_k=16)
        return P.cur(s_dw)

    def fm_stage(src_get, nK, wname, wcols, dst_put, bias_name, act, extra_get=None,
                 src_ready=None):
        nrows = nK * 128
        pw = 4096 // nK * 1
        pw = min(512, (4096 // nK) // 128 * 128)
        assert pw >= 128
        npieces = wcols // pw
        if src_ready is None:
            src_ready = [(s_act, P.cur(s_act)), (s_dve, P.cur(s_dve))]
        for piece in range(npieces):
            slot = ring_slot()
            wdone = load_w_cols(wname, pw * piece, pw * (piece + 1), nrows, slot)
            for j in range(pw // 128):
                fo = piece * (pw // 128) + j
                bank = (ps_st0, ps_st1)[fo % 2]
                bsl = bank[:, 0:512]
                w = [(s_dw, wdone)] + src_ready
                if ST_EVAC[fo % 2] is not None:
                    w.append(ST_EVAC[fo % 2])
                def mm(t, slot=slot, j=j, bsl=bsl, pw=pw, nK=nK, src_get=src_get):
                    last = None
                    for f in range(nK):
                        last = t.matmul(bsl, slot[:, pw * f + 128 * j:pw * f + 128 * (j + 1)],
                                        src_get(f), start=(f == 0), stop=(f == nK - 1))
                    return last
                P.op("tensor", mm, waits=w, inc=s_pe)
                slot_free[slot.name] = [(s_pe, P.cur(s_pe))]
                mmn = P.cur(s_pe)
                if extra_get is not None:
                    def ev(v, fo=fo, bsl=bsl, dst_put=dst_put, bias_name=bias_name,
                           extra_get=extra_get):
                        return v.scalar_tensor_tensor(dst_put(fo), bsl, bias_col(bias_name, fo),
                                                      extra_get(fo), ALU.add, ALU.add)
                    P.op("vector", ev, waits=[(s_pe, mmn)], inc=s_dve)
                    ST_EVAC[fo % 2] = (s_dve, P.cur(s_dve))
                else:
                    def ev(s, fo=fo, bsl=bsl, dst_put=dst_put, bias_name=bias_name, act=act):
                        return s.activation(dst_put(fo), bsl, act, bias=bias_col(bias_name, fo))
                    P.op("scalar", ev, waits=[(s_pe, mmn)], inc=s_act)
                    ST_EVAC[fo % 2] = (s_act, P.cur(s_act))

    def tm_partial(src_get, f_list, wname, r0, brow_off, e_idx, init):
        nch = len(f_list)
        slot = wslot[2]
        wdone = load_w_rows(wname, r0, r0 + 128 * nch, slot)
        for gt in range(16):
            for half in range(2):
                bi = (gt * 2 + half) % 2
                bank = (ps_oa, ps_ob)[bi]
                w = [(s_dw, wdone), (s_dve, eout_guard[bi])]
                def mm(t, gt=gt, half=half, bank=bank, f_list=f_list, src_get=src_get,
                       brow_off=brow_off, nch=nch):
                    last = None
                    for i, f in enumerate(f_list):
                        is_last = (i == nch - 1) and (brow_off is None)
                        last = t.matmul(bank[:, 0:512], src_get(f, gt),
                                        slot[:, 1024 * i + 512 * half:1024 * i + 512 * (half + 1)],
                                        start=(i == 0), stop=is_last)
                    if brow_off is not None:
                        last = t.matmul(bank[:, 0:512], ones_r128_b[0:1, :],
                                        brow_bf[0:1, brow_off + 512 * half:brow_off + 512 * (half + 1)],
                                        start=False, stop=True)
                    return last
                P.op("tensor", mm, waits=w, inc=s_pe)
                mmn = P.cur(s_pe)
                def acc(v, gt=gt, half=half, bank=bank, e_idx=e_idx, init=init):
                    dst = out_acc[:, D * gt + 512 * half:D * gt + 512 * (half + 1)]
                    wc = w4[:, 4 * gt + e_idx:4 * gt + e_idx + 1]
                    if init:
                        return v.tensor_scalar(dst, bank[:, 0:512], wc, 0.0, ALU.mult, ALU.add)
                    return v.scalar_tensor_tensor(dst, bank[:, 0:512], wc, dst, ALU.mult, ALU.add)
                P.op("vector", acc, waits=[(s_pe, mmn)], inc=s_dve)
                eout_guard[bi] = P.cur(s_dve)
        slot_free[slot.name] = [(s_pe, P.cur(s_pe))]

    def ch_get(buf):
        return lambda f: buf[:, 512 * f:512 * (f + 1)]

    def ch_get2(bufs):
        return lambda f: bufs[f // 8][:, 512 * (f % 8):512 * (f % 8 + 1)]

    def ch_put(buf):
        return lambda fo: buf[:, 512 * fo:512 * (fo + 1)]

    def ch_put2(bufs):
        return lambda fo: bufs[fo // 8][:, 512 * (fo % 8):512 * (fo % 8 + 1)]

    def xn_get(tblk):
        return lambda f: xn_own[:, TO * f + 512 * tblk:TO * f + 512 * (tblk + 1)]

    def ln_fm(src_buf, g_name, b_name, dst_buf):
        P.op("vector", lambda v: v.tensor_tensor(chD[:], src_buf[:], src_buf[:], ALU.mult),
             waits=[(s_act, P.cur(s_act)), (s_dve, P.cur(s_dve))], inc=s_dve)
        sqd = P.cur(s_dve)
        def stats(t):
            last = None
            for f in range(NF):
                t.matmul(ps_x[0:1, 0:512], ones_col_bf[:], src_buf[:, 512 * f:512 * (f + 1)],
                         start=(f == 0), stop=(f == NF - 1))
            for f in range(NF):
                last = t.matmul(ps_bc[0:1, 0:512], ones_col_bf[:], chD[:, 512 * f:512 * (f + 1)],
                                start=(f == 0), stop=(f == NF - 1))
            return last
        P.op("tensor", stats, waits=[(s_dve, sqd)], inc=s_pe)
        std = P.cur(s_pe)
        P.op("vector", lambda v: v.tensor_scalar(stat_sb[0:1, 0:512], ps_x[0:1, 0:512], 1.0 / D,
                                                 None, ALU.mult), waits=[(s_pe, std)], inc=s_dve)
        P.op("vector", lambda v: v.tensor_tensor(stat_sb[0:1, 512:1024], stat_sb[0:1, 0:512],
                                                 stat_sb[0:1, 0:512], ALU.mult),
             waits=[(s_dve, P.cur(s_dve))], inc=s_dve)
        P.op("vector", lambda v: v.tensor_scalar(bc_sb[0:1, 0:512], ps_bc[0:1, 0:512], 1.0 / D,
                                                 None, ALU.mult), waits=[(s_dve, P.cur(s_dve))], inc=s_dve)
        P.op("vector", lambda v: v.tensor_tensor(bc_sb[0:1, 0:512], bc_sb[0:1, 0:512],
                                                 stat_sb[0:1, 512:1024], ALU.subtract),
             waits=[(s_dve, P.cur(s_dve))], inc=s_dve)
        vd = P.cur(s_dve)
        P.op("scalar", lambda s: s.activation(bc_sb[0:1, 0:512], bc_sb[0:1, 0:512], AF.Ln,
                                              bias=eps_t[0:1, 0:1]), waits=[(s_dve, vd)], inc=s_act)
        P.op("scalar", lambda s: s.activation(stat_sb[0:1, 512:1024], bc_sb[0:1, 0:512], AF.Exp,
                                              scale=-0.5), waits=[(s_act, P.cur(s_act))], inc=s_act)
        rd = P.cur(s_act)
        P.op("tensor", lambda t: t.matmul(ps_x[:, 0:512], ones_r128_f[0:1, :], stat_sb[0:1, 0:512],
                                          start=True, stop=True), waits=[(s_act, rd)], inc=s_pe)
        P.op("tensor", lambda t: t.matmul(ps_bc[:, 0:512], ones_r128_f[0:1, :],
                                          stat_sb[0:1, 512:1024], start=True, stop=True), inc=s_pe)
        bcd = P.cur(s_pe)
        for f in range(NF):
            P.op("vector", lambda v, f=f: v.tensor_tensor(chD[:, 512 * f:512 * (f + 1)],
                                                          src_buf[:, 512 * f:512 * (f + 1)],
                                                          ps_x[:, 0:512], ALU.subtract),
                 waits=[(s_pe, bcd)], inc=s_dve)
            P.op("vector", lambda v, f=f: v.tensor_tensor(chD[:, 512 * f:512 * (f + 1)],
                                                          chD[:, 512 * f:512 * (f + 1)],
                                                          ps_bc[:, 0:512], ALU.mult),
                 waits=[(s_dve, P.cur(s_dve))], inc=s_dve)
            P.op("vector", lambda v, f=f: v.tensor_scalar(dst_buf[:, 512 * f:512 * (f + 1)],
                                                          chD[:, 512 * f:512 * (f + 1)],
                                                          bias_col(g_name, f), bias_col(b_name, f),
                                                          ALU.mult, ALU.add),
                 waits=[(s_dve, P.cur(s_dve))], inc=s_dve)

    pend_gen = None
    for tblk in range(4):
        fm_stage(xn_get(tblk), NF, "ce_syn_w", D, ch_put(chA), "ce_syn_b", AF.Identity,
                 src_ready=[(s_dve, xn_own_done)] if tblk > 0 else None)
        if pend_gen is not None:
            pend_gen()
            pend_gen = None
        fm_stage(ch_get(chA), NF, "ce_wv", D, ch_put(chB), "ce_bv", AF.Identity)
        fm_stage(ch_get(chB), NF, "ce_wo", D, ch_put(chC), "ce_bo", None, extra_get=ch_get(chA))
        ln_fm(chC, "ce_n1g", "ce_n1b", chA)
        fm_stage(ch_get(chA), NF, "ce_f1w", 2 * D, ch_put2((chC, chD)), "ce_f1b", AF.Relu)
        fm_stage(ch_get2((chC, chD)), 16, "ce_f2w", D, ch_put(chB), "ce_f2b", None,
                 extra_get=ch_get(chA))
        ln_fm(chB, "ce_n2g", "ce_n2b", chC)
        def srcC(f, gt, tblk=tblk):
            ti = gt - 4 * tblk
            return chC[:, 512 * f + 128 * ti:512 * f + 128 * (ti + 1)]
        def tm_partial_blk(src_get, f_list, wname, r0, brow_off, e_idx, init, tblk=tblk,
                           src_ready=None):
            nch = len(f_list)
            slot = ring_slot()
            wdone = load_w_rows(wname, r0, r0 + 128 * nch, slot)
            if src_ready is None:
                src_ready = [(s_act, P.cur(s_act)), (s_dve, P.cur(s_dve))]
            for ti in range(4):
                gt = 4 * tblk + ti
                for half in range(2):
                    bi = (gt * 2 + half) % 2
                    bank = (ps_oa, ps_ob)[bi]
                    w = [(s_dw, wdone), (s_dve, eout_guard[bi])] + src_ready
                    def mm(t, gt=gt, half=half, bank=bank, f_list=f_list, src_get=src_get,
                           brow_off=brow_off, nch=nch, slot=slot):
                        last = None
                        for i, f in enumerate(f_list):
                            is_last = (i == nch - 1) and (brow_off is None)
                            last = t.matmul(bank[:, 0:512], src_get(f, gt),
                                            slot[:, 1024 * i + 512 * half:1024 * i + 512 * (half + 1)],
                                            start=(i == 0), stop=is_last)
                        if brow_off is not None:
                            last = t.matmul(bank[:, 0:512], ones_r128_b[0:1, :],
                                            brow_bf[0:1, brow_off + 512 * half:
                                                    brow_off + 512 * (half + 1)],
                                            start=False, stop=True)
                        return last
                    P.op("tensor", mm, waits=w, inc=s_pe)
                    mmn = P.cur(s_pe)
                    def acc(v, gt=gt, half=half, bank=bank, e_idx=e_idx, init=init):
                        dst = out_acc[:, D * gt + 512 * half:D * gt + 512 * (half + 1)]
                        wc = w4[:, 4 * gt + e_idx:4 * gt + e_idx + 1]
                        if init:
                            return v.tensor_scalar(dst, bank[:, 0:512], wc, 0.0, ALU.mult, ALU.add)
                        return v.scalar_tensor_tensor(dst, bank[:, 0:512], wc, dst, ALU.mult,
                                                      ALU.add)
                    P.op("vector", acc, waits=[(s_pe, mmn)], inc=s_dve)
                    eout_guard[bi] = P.cur(s_dve)
            slot_free[slot.name] = [(s_pe, P.cur(s_pe))]
        def mk_gen(srcC=srcC, tm_partial_blk=tm_partial_blk,
                   sr=None):
            sr = [(s_act, P.cur(s_act)), (s_dve, P.cur(s_dve))]
            def emit_gen():
                tm_partial_blk(srcC, [0, 1, 2, 3], "ce_gen_w", 0, None, 2, True, src_ready=sr)
                tm_partial_blk(srcC, [4, 5, 6, 7], "ce_gen_w", 512, D, 2, False, src_ready=sr)
            return emit_gen
        pend_gen = mk_gen()
    pend_gen()
    pend_gen = None

    def e0_e3(expert):
        w1n, w2n = ("sw_w1", "sw_w2") if expert == 0 else ("ml_w1", "ml_w2")
        for sblk in range(8):
            s1 = wslot[0]
            w1d = load_w_cols(w1n, 512 * sblk, 512 * (sblk + 1), D, s1)
            if expert == 0:
                s3 = wslot[1]
                w3d = load_w_cols("sw_w3", 512 * sblk, 512 * (sblk + 1), D, s3)
            s2 = wslot[2]
            w2d = load_w_rows(w2n, 512 * sblk, 512 * (sblk + 1), s2)
            start_ready = [(s_dve, xn_own_done)] if sblk > 0 else \
                [(s_act, P.cur(s_act)), (s_dve, P.cur(s_dve))]
            for ic in range(4):
                for tq in range(4):
                    it = ic * 4 + tq
                    bank = (ps_st0, ps_st1)[it % 2]
                    w = [(s_dw, w1d)] + start_ready
                    if ST_EVAC[it % 2] is not None:
                        w.append(ST_EVAC[it % 2])
                    def amm(t, s1=s1, ic=ic, tq=tq, bank=bank):
                        last = None
                        for f in range(NF):
                            last = t.matmul(bank[:, 0:512],
                                            s1[:, 512 * f + 128 * ic:512 * f + 128 * (ic + 1)],
                                            xn_own[:, TO * f + 512 * tq:TO * f + 512 * (tq + 1)],
                                            start=(f == 0), stop=(f == NF - 1))
                        return last
                    P.op("tensor", amm, waits=w, inc=s_pe)
                    amn = P.cur(s_pe)
                    if expert == 0:
                        P.op("scalar", lambda s, bank=bank: s.activation(asil[:], bank[:, 0:512],
                                                                         AF.Silu),
                             waits=[(s_pe, amn), (s_dve, P.cur(s_dve))], inc=s_act)
                        sil = P.cur(s_act)
                        def bmm(t, s3=s3, ic=ic, tq=tq, bank=bank):
                            last = None
                            for f in range(NF):
                                last = t.matmul(bank[:, 512:1024],
                                                s3[:, 512 * f + 128 * ic:512 * f + 128 * (ic + 1)],
                                                xn_own[:, TO * f + 512 * tq:TO * f + 512 * (tq + 1)],
                                                start=(f == 0), stop=(f == NF - 1))
                            return last
                        wb = [(s_dw, w3d)]
                        if ST_EVAC[it % 2] is not None:
                            wb.append(ST_EVAC[it % 2])
                        P.op("tensor", bmm, waits=wb, inc=s_pe)
                        bmn = P.cur(s_pe)
                        def gmul(v, ic=ic, tq=tq, bank=bank):
                            return v.tensor_tensor(
                                g_e0[:, 2048 * ic + 512 * tq:2048 * ic + 512 * (tq + 1)],
                                bank[:, 512:1024], asil[:], ALU.mult)
                        P.op("vector", gmul, waits=[(s_pe, bmn), (s_act, sil)], inc=s_dve)
                        ST_EVAC[it % 2] = (s_dve, P.cur(s_dve))
                    else:
                        def gact(s, ic=ic, tq=tq, bank=bank, sblk=sblk):
                            return s.activation(
                                g_e0[:, 2048 * ic + 512 * tq:2048 * ic + 512 * (tq + 1)],
                                bank[:, 0:512], AF.Gelu, bias=bias_col("ml_b1", 4 * sblk + ic))
                        P.op("scalar", gact, waits=[(s_pe, amn)], inc=s_act)
                        ST_EVAC[it % 2] = (s_act, P.cur(s_act))
            slot_free[s1.name] = [(s_pe, P.cur(s_pe))]
            if expert == 0:
                slot_free[s3.name] = [(s_pe, P.cur(s_pe))]
            g_ready = (s_dve, P.cur(s_dve)) if expert == 0 else (s_act, P.cur(s_act))
            for gt in range(16):
                ti = gt % 4
                for half in range(2):
                    bi = (gt * 2 + half) % 2
                    bank = (ps_oa, ps_ob)[bi]
                    bias_here = (expert == 3 and sblk == 7)
                    w = [(s_dw, w2d), (s_dve, eout_guard[bi]), g_ready]
                    def mm(t, gt=gt, half=half, bank=bank, bias_here=bias_here, s2=s2):
                        last = None
                        for ic in range(4):
                            last = t.matmul(bank[:, 0:512],
                                            g_e0[:, 2048 * ic + 128 * gt:2048 * ic + 128 * (gt + 1)],
                                            s2[:, 1024 * ic + 512 * half:1024 * ic + 512 * (half + 1)],
                                            start=(ic == 0), stop=(ic == 3 and not bias_here))
                        if bias_here:
                            last = t.matmul(bank[:, 0:512], ones_r128_b[0:1, :],
                                            brow_bf[0:1, 2 * D + 512 * half:2 * D + 512 * (half + 1)],
                                            start=False, stop=True)
                        return last
                    P.op("tensor", mm, waits=w, inc=s_pe)
                    mmn = P.cur(s_pe)
                    e_idx = 0 if expert == 0 else 3
                    def acc(v, gt=gt, half=half, bank=bank, e_idx=e_idx):
                        dst = out_acc[:, D * gt + 512 * half:D * gt + 512 * (half + 1)]
                        wc = w4[:, 4 * gt + e_idx:4 * gt + e_idx + 1]
                        return v.scalar_tensor_tensor(dst, bank[:, 0:512], wc, dst, ALU.mult,
                                                      ALU.add)
                    P.op("vector", acc, waits=[(s_pe, mmn)], inc=s_dve)
                    eout_guard[bi] = P.cur(s_dve)
            slot_free[s2.name] = [(s_pe, P.cur(s_pe))]

    e0_e3(0)
    e0_e3(3)

    for tblk in range(4):
        fm_stage(xn_get(tblk), NF, "me_eq_w", D, ch_put(chA), "me_eq_b", AF.Identity,
                 src_ready=[(s_dve, xn_own_done)])
        fm_stage(ch_get(chA), NF, "me_wv", D, ch_put(chB), "me_bv", AF.Identity)
        fm_stage(ch_get(chB), NF, "me_wo", D, ch_put(chA), "me_bo", AF.Identity)
        fm_stage(ch_get(chA), NF, "me_c1w", 2 * D, ch_put2((chC, chD)), "me_c1b", AF.Gelu)
        def srcCD(f, gt, tblk=tblk):
            ti = gt - 4 * tblk
            buf = (chC, chD)[f // 8]
            fr = f % 8
            return buf[:, 512 * fr + 128 * ti:512 * fr + 128 * (ti + 1)]
        def tm_partial_blk1(f_list, r0, brow_off, init, tblk=tblk):
            nch = len(f_list)
            slot = ring_slot()
            wdone = load_w_rows("me_c2w", r0, r0 + 128 * nch, slot)
            src_ready = [(s_act, P.cur(s_act)), (s_dve, P.cur(s_dve))]
            for ti in range(4):
                gt = 4 * tblk + ti
                for half in range(2):
                    bi = (gt * 2 + half) % 2
                    bank = (ps_oa, ps_ob)[bi]
                    w = [(s_dw, wdone), (s_dve, eout_guard[bi])] + src_ready
                    def mm(t, gt=gt, half=half, bank=bank, f_list=f_list, brow_off=brow_off,
                           nch=nch, slot=slot, srcCD=srcCD):
                        last = None
                        for i, f in enumerate(f_list):
                            is_last = (i == nch - 1) and (brow_off is None)
                            last = t.matmul(bank[:, 0:512], srcCD(f, gt),
                                            slot[:, 1024 * i + 512 * half:1024 * i + 512 * (half + 1)],
                                            start=(i == 0), stop=is_last)
                        if brow_off is not None:
                            last = t.matmul(bank[:, 0:512], ones_r128_b[0:1, :],
                                            brow_bf[0:1, brow_off + 512 * half:
                                                    brow_off + 512 * (half + 1)],
                                            start=False, stop=True)
                        return last
                    P.op("tensor", mm, waits=w, inc=s_pe)
                    mmn = P.cur(s_pe)
                    def acc(v, gt=gt, half=half, bank=bank):
                        dst = out_acc[:, D * gt + 512 * half:D * gt + 512 * (half + 1)]
                        wc = w4[:, 4 * gt + 1:4 * gt + 2]
                        return v.scalar_tensor_tensor(dst, bank[:, 0:512], wc, dst, ALU.mult,
                                                      ALU.add)
                    P.op("vector", acc, waits=[(s_pe, mmn)], inc=s_dve)
                    eout_guard[bi] = P.cur(s_dve)
            slot_free[slot.name] = [(s_pe, P.cur(s_pe))]
        for qi in range(4):
            tm_partial_blk1(list(range(4 * qi, 4 * qi + 4)), 512 * qi,
                            (0 if qi == 3 else None), False)
    experts_done_dve = P.cur(s_dve)

    if debug:
        P.op("gpsimd", lambda g: g.dma_start(out=dbg["dbg_acc"], in_=out_acc[:]),
             waits=[(s_dve, experts_done_dve)], inc=s_dw, inc_k=16)

    fin_dma = {}
    fsem = [s_xa, s_xb, s_xc, s_xd]
    fsem_base = [P.cur(s) for s in fsem]
    for gt in range(16):
        i = gt % 4
        fa, fx = fin_rs[i], fin_x[i]
        sx = fsem[i]
        w = [(s_cc, 1), (s_dve, experts_done_dve)]
        if gt >= 4:
            w.append((s_do, fin_dma[gt - 4]))
        P.op("sync", lambda sy, gt=gt, fa=fa: sy.dma_start(
            out=fa[:], in_=rs_out[128 * gt:128 * (gt + 1), :]), waits=w, inc=sx, inc_k=16)
        P.op("sync", lambda sy, gt=gt, fx=fx: sy.dma_start(
            out=fx[:], in_=dT["x_own"][128 * gt:128 * (gt + 1), :]), inc=sx, inc_k=16)
        dmad = P.cur(sx)
        P.op("vector", lambda v, fa=fa: v.tensor_tensor(fa[:], fa[:], bp_bc[:], ALU.add),
             waits=[(sx, dmad)], inc=s_dve)
        P.op("vector", lambda v, fa=fa, fx=fx: v.tensor_tensor(fa[:], fa[:], fx[:], ALU.add),
             waits=[(s_dve, P.cur(s_dve))], inc=s_dve)
        P.op("vector", lambda v, fa=fa, gt=gt: v.tensor_tensor(fa[:], fa[:],
                                                               out_acc[:, D * gt:D * (gt + 1)],
                                                               ALU.add),
             waits=[(s_dve, P.cur(s_dve))], inc=s_dve)
        addd = P.cur(s_dve)
        P.op("sync", lambda sy, gt=gt, fa=fa: sy.dma_start(
            out=out[128 * gt:128 * (gt + 1), :], in_=fa[:]),
             waits=[(s_dve, addd)], inc=s_do, inc_k=16)
        fin_dma[gt] = P.cur(s_do)
    total_do = P.cur(s_do)
    P.op("sync", lambda sy: sy.nop(), waits=[(s_do, total_do)])

    import contextlib
    with contextlib.ExitStack() as st:
        handles = {}
        for name in list(P.counts):
            handles[name] = st.enter_context(nc.semaphore(name))
        block = st.enter_context(nc.Block())
        P.emit(block, handles)
    return nc


_NC_CACHE = {}


def _get_nc(debug=False):
    key = bool(debug)
    if key not in _NC_CACHE:
        _NC_CACHE[key] = build_nc(debug=key)
    return _NC_CACHE[key]


def make_in_maps(inputs):
    x = np.ascontiguousarray(np.asarray(inputs["L0_attn_x"], dtype=np.float32))
    B = x.shape[0]
    g = lambda n: np.ascontiguousarray(np.asarray(inputs[n], dtype=np.float32))
    Wq, Wk, Wv, Wp, bp = (g("L0_attn_Wq"), g("L0_attn_Wk"), g("L0_attn_Wv"),
                          g("L0_attn_Wp"), g("L0_attn_bp"))
    F = {k[7:]: g(k) for k in inputs if k.startswith("L0_ffn_")}

    bias_fm = np.zeros((128, BIAS_FM_COLS), dtype=np.float32)
    for name, nch in BIAS_FM_SPECS:
        arr = F[{"me_eq_b": "me_eq_b", "me_bv": "me_bv", "me_bo": "me_bo", "me_c1b": "me_c1b",
                 "ce_syn_b": "ce_syn_b", "ce_bv": "ce_bv", "ce_bo": "ce_bo", "ce_n1g": "ce_n1g",
                 "ce_n1b": "ce_n1b", "ce_f1b": "ce_f1b", "ce_f2b": "ce_f2b", "ce_n2g": "ce_n2g",
                 "ce_n2b": "ce_n2b", "ml_b1": "ml_b1"}[name]]
        for c in range(nch):
            bias_fm[:, BIAS_FM_OFF[name] + c] = arr[128 * c:128 * (c + 1)]
    brow_bf = np.concatenate([F["me_c2b"], F["ce_gen_b"], F["ml_b2"]])[None, :].astype(np.float32)
    brow_f = np.concatenate([bp, F["router_b"], F["load_balancer"]])[None, :].astype(np.float32)

    common = dict(
        router_w=F["router_w"], sw_w1=F["sw_w1"], sw_w3=F["sw_w3"], sw_w2=F["sw_w2"],
        me_eq_w=F["me_eq_w"], me_wv=F["me_wv"], me_wo=F["me_wo"], me_c1w=F["me_c1w"],
        me_c2w=F["me_c2w"], ce_syn_w=F["ce_syn_w"], ce_wv=F["ce_wv"], ce_wo=F["ce_wo"],
        ce_f1w=F["ce_f1w"], ce_f2w=F["ce_f2w"], ce_gen_w=F["ce_gen_w"],
        ml_w1=F["ml_w1"], ml_w2=F["ml_w2"],
        bias_fm=bias_fm, bias_rows_bf=brow_bf, bias_rows_f32=brow_f,
    )
    in_maps = []
    for c in range(8):
        b, pos = c // 2, c % 2
        m = dict(common)
        m["x_full"] = x[b]
        m["x_own"] = np.ascontiguousarray(x[b, TO * pos:TO * (pos + 1)])
        m["wq"] = np.ascontiguousarray(Wq[:, 512 * pos:512 * (pos + 1)])
        m["wk"] = np.ascontiguousarray(Wk[:, 512 * pos:512 * (pos + 1)])
        m["wv"] = np.ascontiguousarray(Wv[:, 512 * pos:512 * (pos + 1)])
        m["wp"] = np.ascontiguousarray(Wp[512 * pos:512 * (pos + 1), :])
        in_maps.append(m)
    return in_maps, B


def kernel(**inputs):
    nc = _get_nc(debug=False)
    in_maps, B = make_in_maps(inputs)
    res = run_bass_kernel_spmd(nc, in_maps, core_ids=list(range(8)))
    outp = np.empty((B, T, D), dtype=np.float32)
    for c in range(8):
        b, pos = c // 2, c % 2
        outp[b, TO * pos:TO * (pos + 1)] = res.results[c]["out"]
    return outp


# version 08
# speedup vs baseline: 1.0660x; 1.0660x over previous
import numpy as np
import concourse.bass as bass
import concourse.mybir as mybir
from concourse.bass_utils import run_bass_kernel_spmd

F32 = mybir.dt.float32
BF16 = mybir.dt.bfloat16
U8 = mybir.dt.uint8
AF = mybir.ActivationFunctionType
ALU = mybir.AluOpType
AX = mybir.AxisListType

T = 4096
TO = 2048
D = 1024
NF = 8
NPAIR = 4
NQB = 8
I_FF = 4096
EPS = 1e-5

BIAS_FM_SPECS = [
    ("me_eq_b", 8), ("me_bv", 8), ("me_bo", 8), ("me_c1b", 16),
    ("ce_syn_b", 8), ("ce_bv", 8), ("ce_bo", 8), ("ce_n1g", 8), ("ce_n1b", 8),
    ("ce_f1b", 16), ("ce_f2b", 8), ("ce_n2g", 8), ("ce_n2b", 8), ("ml_b1", 32),
]
BIAS_FM_OFF = {}
_o = 0
for _n, _k in BIAS_FM_SPECS:
    BIAS_FM_OFF[_n] = _o
    _o += _k
BIAS_FM_COLS = _o

ENGINES = ("tensor", "vector", "scalar", "gpsimd", "sync")


class Prog:
    def __init__(self):
        self.steps = {e: [] for e in ENGINES}
        self.counts = {}

    def sem(self, name):
        self.counts[name] = 0
        return name

    def cur(self, name):
        return self.counts[name]

    def op(self, eng, fn, waits=(), inc=None, inc_k=1):
        wl = tuple((s, v) for (s, v) in waits if v > 0)
        if inc is not None:
            self.counts[inc] += inc_k
        self.steps[eng].append((fn, wl, inc, inc_k))

    def emit(self, block, handles):
        def mk(eng_name):
            def _(eng):
                for fn, waits, inc, inc_k in self.steps[eng_name]:
                    for s, v in waits:
                        eng.wait_ge(handles[s], v)
                    ins = fn(eng)
                    if inc is not None:
                        assert ins is not None, f"{eng_name} step with inc={inc} returned None"
                        ins.then_inc(handles[inc], inc_k)
            return _
        block.tensor(mk("tensor"))
        block.vector(mk("vector"))
        block.scalar(mk("scalar"))
        block.gpsimd(mk("gpsimd"))
        block.sync(mk("sync"))


def build_nc(debug=False):
    nc = bass.Bass("TRN2", target_bir_lowering=False)

    dT = {}

    def din(name, shape, dtype=F32):
        dT[name] = nc.dram_tensor(name, list(shape), dtype, kind="ExternalInput").ap()

    din("x_full", (T, D))
    din("x_own", (TO, D))
    din("wq", (D, 512)); din("wk", (D, 512)); din("wv", (D, 512)); din("wp", (512, D))
    din("router_w", (D, 4))
    din("sw_w1", (D, I_FF)); din("sw_w3", (D, I_FF)); din("sw_w2", (I_FF, D))
    din("me_eq_w", (D, D)); din("me_wv", (D, D)); din("me_wo", (D, D))
    din("me_c1w", (D, 2 * D)); din("me_c2w", (2 * D, D))
    din("ce_syn_w", (D, D)); din("ce_wv", (D, D)); din("ce_wo", (D, D))
    din("ce_f1w", (D, 2 * D)); din("ce_f2w", (2 * D, D)); din("ce_gen_w", (D, D))
    din("ml_w1", (D, I_FF)); din("ml_w2", (I_FF, D))
    din("bias_fm", (128, BIAS_FM_COLS))
    din("bias_rows_bf", (1, 3 * D))
    din("bias_rows_f32", (1, D + 8))

    out = nc.dram_tensor("out", [TO, D], F32, kind="ExternalOutput").ap()
    rs_in = nc.dram_tensor("rs_in", [T, D], F32, kind="Internal").ap()
    rs_out = nc.dram_tensor("rs_out", [TO, D], F32, kind="Internal").ap()

    dbg = {}
    if debug:
        for nm, shape in [("dbg_xn", (128, NF * T)), ("dbg_hout", (128, NPAIR * T)),
                          ("dbg_w4", (128, 64)), ("dbg_acc", (128, 16 * D)),
                          ("dbg_pp", (T, D)), ("dbg_xnown", (128, NF * TO)),
                          ("dbg_lg", (128, 64))]:
            dbg[nm] = nc.dram_tensor(nm, list(shape), F32, kind="ExternalOutput").ap()

    ARENA_KB = 206
    nc.alloc_sbuf_tensor("arena", [128, ARENA_KB * 1024], U8)
    base = None
    for alloc in nc.m.functions[0].allocations:
        if hasattr(alloc, "memorylocations"):
            for ml in alloc.memorylocations:
                if ml.name == "arena":
                    base = ml.addr
    assert base is not None
    LIMIT = base + ARENA_KB * 1024
    cur = [base]

    def sb(name, shape, dtype, at=None):
        a = cur[0] if at is None else at
        a = (a + 31) // 32 * 32
        t = nc.alloc_sbuf_tensor_at(name, list(shape), dtype, offset=a)
        lo, hi = t.manual_sbuf_range
        assert hi <= LIMIT, f"{name}: {hi} > {LIMIT}"
        if at is None:
            cur[0] = hi
        return t

    R1 = cur[0]
    xn_fm = sb("xn_fm", [128, NF * T], BF16, at=R1)
    out_acc = sb("out_acc", [128, 16 * D], F32, at=R1)
    cur[0] = R1 + 65536
    R2 = cur[0]
    vt = sb("vt", [128, 8 * 32 * 65], BF16, at=R2)
    xn_own = sb("xn_own", [128, NF * TO], BF16, at=R2)
    cur[0] = R2 + 33408
    R3 = cur[0]
    headout = sb("headout", [128, NPAIR * T], BF16, at=R3)
    chA = sb("chA", [128, 4096], BF16, at=R3)
    chB = sb("chB", [128, 4096], BF16, at=R3 + 8192)
    chC = sb("chC", [128, 4096], BF16, at=R3 + 16384)
    chD = sb("chD", [128, 4096], BF16, at=R3 + 24576)
    g_e0 = sb("g_e0", [128, 4 * TO], BF16, at=R3)
    cur[0] = R3 + 32768
    R4 = cur[0]
    k_sb = sb("k_sb", [128, T], BF16, at=R4)
    q_sb = sb("q_sb", [128, T], BF16, at=R4 + 8192)
    xt0 = sb("xt0", [128, D], F32, at=R4)
    xt1 = sb("xt1", [128, D], F32, at=R4 + 4096)
    cen_sb = sb("cen_sb", [128, D], F32, at=R4 + 8192)
    sq_sb = sb("sq_sb", [128, D], BF16, at=R4 + 12288)
    fin_rs = [sb(f"fin_rs{i}", [128, D], F32, at=R3 + 4096 * i) for i in range(4)]
    fin_x = [sb(f"fin_x{i}", [128, D], F32, at=R3 + 16384 + 4096 * i) for i in range(4)]
    cur[0] = R4 + 16384
    R5 = cur[0]
    pt0 = sb("pt0", [128, 1024], BF16, at=R5)
    pt1 = sb("pt1", [128, 1024], BF16, at=R5 + 2048)
    cur[0] = R5 + 4096
    R6 = cur[0]
    wslot = [sb(f"wslot{i}", [128, 4096], BF16, at=R6 + 8192 * i) for i in range(3)]
    pt2 = sb("pt2", [128, 1024], BF16, at=R6 + 16384)
    pt3 = sb("pt3", [128, 1024], BF16, at=R6 + 16384 + 2048)
    oxt0 = sb("oxt0", [128, D], F32, at=R6)
    oxt1 = sb("oxt1", [128, D], F32, at=R6 + 4096)
    ocen = sb("ocen", [128, D], F32, at=R6 + 8192)
    osq = sb("osq", [128, D], BF16, at=R6 + 12288)
    cur[0] = R6 + 24576
    ident = sb("ident", [128, 128], F32)
    iota_f = sb("iota_f", [128, 512], F32, at=R4)
    iota_p = sb("iota_p", [128, 512], F32, at=R4 + 2048)
    masks = sb("masks", [128, 4 * 512], BF16)
    ones_r128_f = sb("ones_r128_f", [1, 128], F32)
    ones_r128_b = sb("ones_r128_b", [1, 128], BF16)
    ones_col_bf = sb("ones_col_bf", [128, 1], BF16)
    bias_fm_sb = sb("bias_fm_sb", [128, BIAS_FM_COLS], F32)
    brow_bf = sb("brow_bf", [1, 3 * D], BF16)
    brow_f = sb("brow_f", [1, D + 8], F32)
    bp_bc = sb("bp_bc", [128, D], F32)
    rbc = sb("rbc", [128, 4], F32)
    w4 = sb("w4", [128, 64], F32)
    negC = sb("negC", [128, 1], F32)
    cm_sb = sb("cm_sb", [1, 8], F32)
    stat_sb = sb("stat_sb", [1, 1024], F32)
    bc_sb = sb("bc_sb", [128, 512], F32)
    vfm_tmp = sb("vfm_tmp", [128, 512], F32)
    asil = sb("asil", [128, 512], F32)
    r_tiny = sb("r_tiny", [128, 8], F32)
    pp0 = sb("pp0", [128, D], F32, at=R4)
    pp1 = sb("pp1", [128, D], F32, at=R4 + 4096)
    eps_t = sb("eps_t", [1, 1], F32)
    epsP = sb("epsP", [128, 1], F32)
    dbg_lg_sb = sb("dbg_lg_sb", [128, 64], F32)
    bnst = sb("bnst", [128, 32], F32)
    xn32 = sb("xn32", [128, D], F32, at=R5)
    rw32 = sb("rw32", [128, 32], F32)
    assert cur[0] <= LIMIT, f"arena overflow: {cur[0] - base} > {ARENA_KB * 1024}"

    ps_st0 = nc.alloc_psum_tensor("ps_st0", [128, 1024], F32)
    ps_st1 = nc.alloc_psum_tensor("ps_st1", [128, 1024], F32)
    ps_oa = nc.alloc_psum_tensor("ps_oa", [128, 512], F32)
    ps_ob = nc.alloc_psum_tensor("ps_ob", [128, 512], F32)
    ps_bc = nc.alloc_psum_tensor("ps_bc", [128, 512], F32)
    ps_x = nc.alloc_psum_tensor("ps_x", [128, 512], F32)

    P = Prog()
    s_dx = P.sem("s_dx"); s_dw = P.sem("s_dw"); s_do = P.sem("s_do")
    s_pe = P.sem("s_pe"); s_act = P.sem("s_act"); s_dve = P.sem("s_dve")
    s_qk = P.sem("s_qk"); s_exp = P.sem("s_exp"); s_msk = P.sem("s_msk")
    s_av = P.sem("s_av"); s_cc = P.sem("s_cc"); s_po = P.sem("s_po")
    s_xa = P.sem("s_xa"); s_xb = P.sem("s_xb")
    s_xc = P.sem("s_xc"); s_xd = P.sem("s_xd")

    def bias_col(name, chunk):
        o = BIAS_FM_OFF[name] + chunk
        return bias_fm_sb[:, o:o + 1]

    P.op("gpsimd", lambda g: g.iota(iota_f[:], pattern=[[1, 512]], base=0, channel_multiplier=0,
                                    allow_small_or_imprecise_dtypes=True), inc=s_po)
    P.op("gpsimd", lambda g: g.iota(iota_p[:], pattern=[[0, 512]], base=0, channel_multiplier=1,
                                    allow_small_or_imprecise_dtypes=True), inc=s_po)
    P.op("gpsimd", lambda g: g.memset(vt[:], 1.0), inc=s_po)
    P.op("gpsimd", lambda g: g.memset(ones_r128_f[:], 1.0), inc=s_po)
    P.op("gpsimd", lambda g: g.memset(ones_r128_b[:], 1.0), inc=s_po)
    P.op("gpsimd", lambda g: g.memset(ones_col_bf[:], 1.0), inc=s_po)
    P.op("gpsimd", lambda g: g.memset(eps_t[:], EPS), inc=s_po)
    P.op("gpsimd", lambda g: g.memset(epsP[:], EPS), inc=s_po)
    P.op("gpsimd", lambda g: g.dma_start(out=bias_fm_sb[:], in_=dT["bias_fm"]), inc=s_dw, inc_k=16)
    P.op("gpsimd", lambda g: g.dma_start(out=brow_bf[:], in_=dT["bias_rows_bf"]), inc=s_dw, inc_k=16)
    P.op("gpsimd", lambda g: g.dma_start(out=brow_f[:], in_=dT["bias_rows_f32"]), inc=s_dw, inc_k=16)
    brows_ld = P.cur(s_dw)
    P.op("vector", lambda v: v.tensor_tensor(ident[:], iota_f[:, 0:128], iota_p[:, 0:128],
                                             ALU.is_equal), waits=[(s_po, 2)], inc=s_dve)
    for mi in range(4):
        P.op("vector", lambda v, mi=mi: v.tensor_scalar(bc_sb[:], iota_f[:], 128.0 * mi, None,
                                                        ALU.subtract),
             waits=[(s_po, 2), (s_dve, P.cur(s_dve))], inc=s_dve)
        P.op("vector", lambda v, mi=mi: v.tensor_tensor(masks[:, 512 * mi:512 * (mi + 1)],
                                                        bc_sb[:], iota_p[:], ALU.is_ge),
             waits=[(s_dve, P.cur(s_dve))], inc=s_dve)
    def bpbc_mm(t):
        t.matmul(ps_bc[:, 0:512], ones_r128_f[0:1, :], brow_f[0:1, 0:512], start=True, stop=True)
        return t.matmul(ps_x[:, 0:512], ones_r128_f[0:1, :], brow_f[0:1, 512:1024], start=True, stop=True)
    P.op("tensor", bpbc_mm, waits=[(s_dw, brows_ld), (s_po, 8)], inc=s_pe)
    P.op("vector", lambda v: v.tensor_copy(bp_bc[:, 0:512], ps_bc[:]),
         waits=[(s_pe, P.cur(s_pe))], inc=s_dve)
    P.op("vector", lambda v: v.tensor_copy(bp_bc[:, 512:1024], ps_x[:]), inc=s_dve)
    bp_done = P.cur(s_dve)
    P.op("tensor", lambda t: t.matmul(ps_bc[:, 0:8], ones_r128_f[0:1, :], brow_f[0:1, 1024:1032],
                                      start=True, stop=True), waits=[(s_dve, bp_done)], inc=s_pe)
    P.op("vector", lambda v: v.tensor_copy(rbc[:], ps_bc[:, 0:4]),
         waits=[(s_pe, P.cur(s_pe))], inc=s_dve)
    P.op("vector", lambda v: v.tensor_tensor(rbc[:], rbc[:], ps_bc[:, 4:8], ALU.add),
         waits=[(s_dve, P.cur(s_dve))], inc=s_dve)
    setup_dve = P.cur(s_dve)
    setup_pe = P.cur(s_pe)

    ln_done = {}
    xp_evac = {}
    for tt in range(32):
        xt = xt0 if tt % 2 == 0 else xt1
        w = [(s_dve, ln_done[tt - 2])] if tt >= 2 else [(s_dve, setup_dve), (s_pe, setup_pe)]
        P.op("sync", lambda sy, tt=tt, xt=xt: sy.dma_start(
            out=xt[:], in_=dT["x_full"][128 * tt:128 * (tt + 1), :]), waits=w, inc=s_dx, inc_k=16)
        q0 = 16 * (tt % 2)
        P.op("vector", lambda v, xt=xt, q0=q0: v.bn_stats(bnst[:, q0:q0 + 6], xt[:, 0:512]),
             waits=[(s_dx, P.cur(s_dx))], inc=s_dve)
        P.op("vector", lambda v, xt=xt, q0=q0: v.bn_stats(bnst[:, q0 + 6:q0 + 12], xt[:, 512:1024]),
             inc=s_dve)
        P.op("vector", lambda v, q0=q0: v.bn_aggr(bnst[:, q0 + 12:q0 + 14], bnst[:, q0:q0 + 12]),
             waits=[(s_dve, P.cur(s_dve))], inc=s_dve)
        bna = P.cur(s_dve)
        P.op("scalar", lambda s, q0=q0: s.activation(bnst[:, q0 + 14:q0 + 15], bnst[:, q0 + 13:q0 + 14],
                                                     AF.Ln, bias=epsP[:]),
             waits=[(s_dve, bna), (s_po, 8)], inc=s_act)
        P.op("scalar", lambda s, q0=q0: s.activation(bnst[:, q0 + 15:q0 + 16], bnst[:, q0 + 14:q0 + 15],
                                                     AF.Exp, scale=-0.5),
             waits=[(s_act, P.cur(s_act))], inc=s_act)
        P.op("vector", lambda v, xt=xt, q0=q0: v.tensor_scalar(cen_sb[:], xt[:], bnst[:, q0 + 12:q0 + 13],
                                                               bnst[:, q0 + 15:q0 + 16],
                                                               ALU.subtract, ALU.mult),
             waits=[(s_act, P.cur(s_act))], inc=s_dve)
        scaled = P.cur(s_dve)
        for f in range(NF):
            gi = tt * NF + f
            bank = (ps_oa, ps_ob, ps_bc)[gi % 3]
            w = [(s_dve, scaled)]
            if gi >= 3:
                w.append((s_dve, xp_evac[gi - 3]))
            elif gi == 0:
                w.append((s_dve, setup_dve))
                w.append((s_pe, setup_pe))
            P.op("tensor", lambda t, f=f, bank=bank: t.transpose(
                bank[:, 0:128], cen_sb[:, 128 * f:128 * (f + 1)], ident[:]), waits=w, inc=s_pe)
            P.op("vector", lambda v, f=f, tt=tt, bank=bank: v.tensor_copy(
                xn_fm[:, T * f + 128 * tt:T * f + 128 * (tt + 1)], bank[:, 0:128]),
                 waits=[(s_pe, P.cur(s_pe))], inc=s_dve)
            xp_evac[gi] = P.cur(s_dve)
        ln_done[tt] = P.cur(s_dve)
    p1_dve = P.cur(s_dve)

    wv_ld = {}
    vproj_done = {}
    vt_evac = {}
    vfm_evac = {}

    def emit_vload(p):
        slot = wslot[p % 3]
        w = [] if p < 3 else [(s_pe, vproj_done[p - 3])]
        P.op("gpsimd", lambda g, p=p, slot=slot: g.dma_start(
            out=slot[:, 0:1024].rearrange("p (a j) -> p a j", a=8),
            in_=dT["wv"][:, 128 * p:128 * (p + 1)].rearrange("(a p) j -> p a j", p=128)),
             waits=w, inc=s_dw, inc_k=16)
        wv_ld[p] = P.cur(s_dw)

    emit_vload(0)
    emit_vload(1)
    for p in range(NPAIR):
        slot = wslot[p % 3]
        hA, hB = 2 * p, 2 * p + 1
        for tt in range(8):
            it = p * 8 + tt
            bank = (ps_st0, ps_st1)[tt % 2]
            w = [(s_dw, wv_ld[p]), (s_dve, p1_dve)]
            if it >= 2:
                w.append((s_act, vfm_evac[it - 2]))
            def vmm(t, slot=slot, tt=tt, bank=bank):
                last = None
                for f in range(NF):
                    last = t.matmul(bank[:, 0:512], slot[:, 128 * f:128 * (f + 1)],
                                    xn_fm[:, T * f + 512 * tt:T * f + 512 * (tt + 1)],
                                    start=(f == 0), stop=(f == NF - 1))
                return last
            P.op("tensor", vmm, waits=w, inc=s_pe)
            mmn = P.cur(s_pe)
            P.op("scalar", lambda s, bank=bank: s.copy(vfm_tmp[:], bank[:, 0:512]),
                 waits=[(s_pe, mmn)], inc=s_act)
            vfm_evac[it] = P.cur(s_act)
            for j4 in range(4):
                kt = tt * 4 + j4
                gk = p * 32 + kt
                tb = (ps_oa, ps_ob, ps_bc)[gk % 3]
                w = [(s_act, vfm_evac[it])]
                if gk >= 3:
                    w.append((s_dve, vt_evac[gk - 3]))
                P.op("tensor", lambda t, j4=j4, tb=tb: t.transpose(
                    tb[:, 0:128], vfm_tmp[:, 128 * j4:128 * (j4 + 1)], ident[:]), waits=w, inc=s_pe)
                tpn = P.cur(s_pe)
                def evv(v, kt=kt, tb=tb, hA=hA, hB=hB):
                    v.tensor_copy(vt[:, 65 * (32 * hA + kt):65 * (32 * hA + kt) + 64], tb[:, 0:64])
                    return v.tensor_copy(vt[:, 65 * (32 * hB + kt):65 * (32 * hB + kt) + 64],
                                         tb[:, 64:128])
                P.op("vector", evv, waits=[(s_pe, tpn), (s_po, 8)], inc=s_dve)
                vt_evac[gk] = P.cur(s_dve)
        vproj_done[p] = P.cur(s_pe)
        if p + 2 < NPAIR:
            emit_vload(p + 2)

    unit = 0
    exp_at = {}
    av_at = {}
    kq_proj_done = {}
    oa_free = {}
    raw_used = {}
    bc_busy = {}
    pending_norm = None
    for p in range(NPAIR):
        hA, hB = 2 * p, 2 * p + 1
        sK, sQ = wslot[0], wslot[1]
        wld = []
        if p == 0:
            wld = [(s_pe, vproj_done[min(3, NPAIR - 1)])]
        else:
            wld = [(s_pe, kq_proj_done[p - 1])]
        P.op("gpsimd", lambda g, p=p, sK=sK: g.dma_start(
            out=sK[:, 0:1024].rearrange("p (a j) -> p a j", a=8),
            in_=dT["wk"][:, 128 * p:128 * (p + 1)].rearrange("(a p) j -> p a j", p=128)),
             waits=wld, inc=s_dw, inc_k=16)
        P.op("gpsimd", lambda g, p=p, sQ=sQ: g.dma_start(
            out=sQ[:, 0:1024].rearrange("p (a j) -> p a j", a=8),
            in_=dT["wq"][:, 128 * p:128 * (p + 1)].rearrange("(a p) j -> p a j", p=128)),
             waits=wld, inc=s_dw, inc_k=16)
        kq_w = P.cur(s_dw)
        for which, wsl, dst in (("k", sK, k_sb), ("q", sQ, q_sb)):
            for tt in range(8):
                bank = (ps_st0, ps_st1)[tt % 2]
                w = [(s_dw, kq_w), (s_act, P.cur(s_act))]
                if p == 0 and which == "k" and tt < 2:
                    w.append((s_dve, P.cur(s_dve)))
                if p > 0 and which == "k" and tt == 0:
                    w.append((s_av, P.cur(s_av)))
                def kqm(t, wsl=wsl, tt=tt, bank=bank):
                    last = None
                    for f in range(NF):
                        last = t.matmul(bank[:, 0:512], wsl[:, 128 * f:128 * (f + 1)],
                                        xn_fm[:, T * f + 512 * tt:T * f + 512 * (tt + 1)],
                                        start=(f == 0), stop=(f == NF - 1))
                    return last
                P.op("tensor", kqm, waits=w, inc=s_pe)
                mmn = P.cur(s_pe)
                P.op("scalar", lambda s, dst=dst, tt=tt, bank=bank, which=which: s.activation(
                    dst[:, 512 * tt:512 * (tt + 1)], bank[:, 0:512], AF.Identity,
                    scale=(0.125 if which == "q" else 1.0)), waits=[(s_pe, mmn)], inc=s_act)
        kq_act = P.cur(s_act)
        kq_proj_done[p] = P.cur(s_pe)
        P.op("vector", lambda v: v.tensor_reduce(r_tiny[:, 5:6], k_sb[:], AX.X, ALU.max,
                                                 apply_absolute_value=True),
             waits=[(s_act, kq_act)], inc=s_dve)
        P.op("vector", lambda v: v.tensor_reduce(r_tiny[:, 6:7], q_sb[:], AX.X, ALU.max,
                                                 apply_absolute_value=True), inc=s_dve)
        mx = P.cur(s_dve)
        P.op("tensor", lambda t: t.transpose(ps_x[0:1, 0:128], r_tiny[:, 5:6], ident[:]),
             waits=[(s_dve, mx)], inc=s_pe)
        P.op("tensor", lambda t: t.transpose(ps_x[0:1, 128:256], r_tiny[:, 6:7], ident[:]),
             inc=s_pe)
        tpd = P.cur(s_pe)
        P.op("vector", lambda v: v.tensor_reduce(cm_sb[0:1, 0:1], ps_x[0:1, 0:128], AX.X, ALU.max),
             waits=[(s_pe, tpd)], inc=s_dve)
        P.op("vector", lambda v: v.tensor_reduce(cm_sb[0:1, 1:2], ps_x[0:1, 128:256], AX.X, ALU.max),
             inc=s_dve)
        P.op("vector", lambda v: v.tensor_tensor(cm_sb[0:1, 2:3], cm_sb[0:1, 0:1], cm_sb[0:1, 1:2],
                                                 ALU.mult), waits=[(s_dve, P.cur(s_dve))], inc=s_dve)
        P.op("vector", lambda v: v.tensor_scalar(cm_sb[0:1, 3:4], cm_sb[0:1, 2:3], -64.0, 40.0,
                                                 ALU.mult, ALU.add),
             waits=[(s_dve, P.cur(s_dve))], inc=s_dve)
        P.op("tensor", lambda t: t.matmul(ps_x[:, 256:257], ones_r128_f[0:1, :], cm_sb[0:1, 3:4],
                                          start=True, stop=True),
             waits=[(s_dve, P.cur(s_dve))], inc=s_pe)
        P.op("vector", lambda v: v.tensor_copy(negC[:], ps_x[:, 256:257]),
             waits=[(s_pe, P.cur(s_pe))], inc=s_dve)
        negc_done = P.cur(s_dve)

        pending_norm = pending_norm if p > 0 else None
        for qb in range(NQB):
            ncq = 4 * (qb + 1)
            qb_prev = (qb - 1) if qb > 0 else (NQB - 1 if p > 0 else None)
            pending_av = None
            for kc in range(ncq):
                if kc == 4 and pending_norm is not None:
                    pending_norm()
                    pending_norm = None
                stb = (ps_st0, ps_st1)[unit % 2]
                ptb = (pt0, pt1, pt2, pt3)[unit % 4]
                band = kc >= ncq - 4
                w = [(s_dve, negc_done)]
                if unit >= 2:
                    w.append((s_exp, exp_at[unit - 2]))
                def qk(t, kc=kc, qb=qb, stb=stb):
                    t.matmul(stb[:, 0:512], k_sb[0:64, 128 * kc:128 * (kc + 1)],
                             q_sb[0:64, 512 * qb:512 * (qb + 1)], start=True, stop=True,
                             tile_position=(0, 0))
                    return t.matmul(stb[:, 512:1024], k_sb[64:128, 128 * kc:128 * (kc + 1)],
                                    q_sb[64:128, 512 * qb:512 * (qb + 1)], start=True, stop=True,
                                    tile_position=(64, 0))
                P.op("tensor", qk, waits=w, inc=s_qk)
                w = [(s_qk, P.cur(s_qk))]
                if unit >= 4:
                    w.append((s_av, av_at[unit - 4]))
                P.op("scalar", lambda s, stb=stb, ptb=ptb: s.activation(ptb[:], stb[:], AF.Exp,
                                                                       bias=negC[:]),
                     waits=w, inc=s_exp)
                exp_at[unit] = P.cur(s_exp)
                if band:
                    mi = kc - (ncq - 4)
                    def msk(v, ptb=ptb, mi=mi):
                        v.tensor_tensor(ptb[:, 0:512], ptb[:, 0:512],
                                        masks[:, 512 * mi:512 * (mi + 1)], ALU.mult)
                        return v.tensor_tensor(ptb[:, 512:1024], ptb[:, 512:1024],
                                               masks[:, 512 * mi:512 * (mi + 1)], ALU.mult)
                    P.op("vector", msk, waits=[(s_exp, exp_at[unit])], inc=s_msk)
                w = [(s_exp, exp_at[unit])]
                if band:
                    w.append((s_msk, P.cur(s_msk)))
                if kc == 0 and qb_prev is not None:
                    w.append((s_act, oa_free[qb_prev]))
                def av(t, kc=kc, ptb=ptb, hA=hA, hB=hB, first=(kc == 0), last=(kc == ncq - 1)):
                    t.matmul(ps_oa[0:65, :], vt[:, 65 * (32 * hA + kc):65 * (32 * hA + kc) + 65],
                             ptb[:, 0:512], start=first, stop=last)
                    return t.matmul(ps_ob[0:65, :], vt[:, 65 * (32 * hB + kc):65 * (32 * hB + kc) + 65],
                                    ptb[:, 512:1024], start=first, stop=last)
                if pending_av is not None:
                    fn_av, w_av, u_av = pending_av
                    P.op("tensor", fn_av, waits=w_av, inc=s_av)
                    av_at[u_av] = P.cur(s_av)
                pending_av = (av, w, unit)
                unit += 1
            if pending_av is not None:
                fn_av, w_av, u_av = pending_av
                P.op("tensor", fn_av, waits=w_av, inc=s_av)
                av_at[u_av] = P.cur(s_av)
                pending_av = None
            if pending_norm is not None:
                pending_norm()
                pending_norm = None
            w_raw = [(s_av, P.cur(s_av))]
            if qb_prev is not None:
                w_raw.append((s_dve, raw_used[qb_prev]))
            P.op("scalar", lambda s: s.copy(vfm_tmp[0:65, :], ps_oa[0:65, :]),
                 waits=w_raw, inc=s_act)
            P.op("scalar", lambda s: s.copy(asil[0:65, :], ps_ob[0:65, :]),
                 waits=[(s_act, P.cur(s_act))], inc=s_act)
            raw_ev = P.cur(s_act)
            oa_free[qb] = raw_ev
            P.op("vector", lambda v: v.reciprocal(stat_sb[0:1, 0:512], vfm_tmp[64:65, :]),
                 waits=[(s_act, raw_ev)], inc=s_dve)
            P.op("vector", lambda v: v.reciprocal(stat_sb[0:1, 512:1024], asil[64:65, :]),
                 waits=[(s_dve, P.cur(s_dve))], inc=s_dve)
            rec = P.cur(s_dve)

            def mk_norm(p=p, qb=qb, rec=rec, wprev=(bc_busy.get(qb_prev) if qb_prev is not None else None)):
                def emit_norm():
                    wbc = [(s_dve, rec)]
                    if wprev is not None:
                        wbc.append((s_dve, wprev))
                    P.op("tensor", lambda t: t.matmul(ps_bc[0:64, :], ones_r128_f[0:1, 0:64],
                                                      stat_sb[0:1, 0:512], start=True, stop=True),
                         waits=wbc, inc=s_pe)
                    P.op("tensor", lambda t: t.matmul(ps_bc[64:128, :], ones_r128_f[0:1, 0:64],
                                                      stat_sb[0:1, 512:1024], start=True, stop=True,
                                                      tile_position=(0, 64)), inc=s_pe)
                    bcd = P.cur(s_pe)
                    def norm(v, p=p, qb=qb):
                        v.tensor_tensor(headout[0:64, T * p + 512 * qb:T * p + 512 * (qb + 1)],
                                        vfm_tmp[0:64, :], ps_bc[0:64, :], ALU.mult)
                        return v.tensor_tensor(headout[64:128, T * p + 512 * qb:T * p + 512 * (qb + 1)],
                                               asil[0:64, :], ps_bc[64:128, :], ALU.mult)
                    P.op("vector", norm, waits=[(s_pe, bcd)], inc=s_dve)
                    raw_used[qb] = P.cur(s_dve)
                    bc_busy[qb] = P.cur(s_dve)
                return emit_norm
            pending_norm = mk_norm()
    if pending_norm is not None:
        pending_norm()
        pending_norm = None
    attn_dve = P.cur(s_dve)
    attn_av = P.cur(s_av)
    attn_pe = P.cur(s_pe)

    if debug:
        P.op("gpsimd", lambda g: g.dma_start(out=dbg["dbg_xn"], in_=xn_fm[:]),
             waits=[(s_dve, p1_dve)], inc=s_dw, inc_k=16)
        P.op("gpsimd", lambda g: g.dma_start(out=dbg["dbg_hout"], in_=headout[:]),
             waits=[(s_dve, attn_dve)], inc=s_dw, inc_k=16)

    P.op("gpsimd", lambda g: g.dma_start(
        out=wslot[2][:, 0:4096].rearrange("p (a n) -> p a n", a=4),
        in_=dT["wp"].rearrange("(a p) n -> p a n", p=128)),
         waits=[(s_av, attn_av)], inc=s_dw, inc_k=16)
    wp_ld = P.cur(s_dw)
    pp_ev = {}
    st_cnt = {}
    for tt in range(32):
        ppb = pp0 if tt % 2 == 0 else pp1
        for half in range(2):
            hb = (ps_oa, ps_ob)[half]
            w = [(s_dw, wp_ld), (s_dve, attn_dve)]
            if tt >= 1:
                w.append((s_dve, pp_ev[(tt - 1, half)]))
            def ppm(t, tt=tt, half=half, hb=hb):
                last = None
                for c in range(4):
                    last = t.matmul(hb[:, 0:512],
                                    headout[:, T * c + 128 * tt:T * c + 128 * (tt + 1)],
                                    wslot[2][:, 1024 * c + 512 * half:1024 * c + 512 * (half + 1)],
                                    start=(c == 0), stop=(c == 3))
                return last
            P.op("tensor", ppm, waits=w, inc=s_pe)
            mmn = P.cur(s_pe)
            w = [(s_pe, mmn)]
            if tt >= 2:
                w.append((s_dx, st_cnt[tt - 2]))
            P.op("vector", lambda v, ppb=ppb, half=half, hb=hb: v.tensor_copy(
                ppb[:, 512 * half:512 * (half + 1)], hb[:, 0:512]), waits=w, inc=s_dve)
            pp_ev[(tt, half)] = P.cur(s_dve)
        P.op("sync", lambda sy, tt=tt, ppb=ppb: sy.dma_start(
            out=rs_in[128 * tt:128 * (tt + 1), :], in_=ppb[:]),
             waits=[(s_dve, pp_ev[(tt, 1)])], inc=s_dx, inc_k=16)
        st_cnt[tt] = P.cur(s_dx)
    rs_ready = P.cur(s_dx)
    P.op("gpsimd", lambda g: g.collective_compute(
        "ReduceScatter", ALU.add, replica_groups=[[0, 1], [2, 3], [4, 5], [6, 7]],
        ins=[rs_in.opt()], outs=[rs_out.opt()]), waits=[(s_dx, rs_ready)], inc=s_cc)
    if debug:
        P.op("gpsimd", lambda g: g.dma_start(out=dbg["dbg_pp"], in_=rs_in),
             waits=[(s_dx, rs_ready)], inc=s_dw, inc_k=16)

    P.op("gpsimd", lambda g: g.dma_start(
        out=rw32[:, 0:32].rearrange("p (a e) -> p a e", a=8),
        in_=dT["router_w"].rearrange("(a p) e -> p a e", p=128)),
         waits=[(s_av, attn_av)], inc=s_dw, inc_k=16)
    rw_ld = P.cur(s_dw)
    oln = {}
    oxp = {}
    for tt in range(16):
        xt = oxt0 if tt % 2 == 0 else oxt1
        w = [(s_pe, kq_proj_done[NPAIR - 1])]
        if tt >= 2:
            w.append((s_dve, oln[tt - 2]))
        sx = s_xa if tt % 2 == 0 else s_xb
        P.op("sync", lambda sy, tt=tt, xt=xt: sy.dma_start(
            out=xt[:], in_=dT["x_own"][128 * tt:128 * (tt + 1), :]), waits=w, inc=sx, inc_k=16)
        q0 = 16 * (tt % 2)
        P.op("vector", lambda v, xt=xt, q0=q0: v.bn_stats(bnst[:, q0:q0 + 6], xt[:, 0:512]),
             waits=[(sx, P.cur(sx)), (s_dve, attn_dve)], inc=s_dve)
        P.op("vector", lambda v, xt=xt, q0=q0: v.bn_stats(bnst[:, q0 + 6:q0 + 12], xt[:, 512:1024]),
             inc=s_dve)
        P.op("vector", lambda v, q0=q0: v.bn_aggr(bnst[:, q0 + 12:q0 + 14], bnst[:, q0:q0 + 12]),
             waits=[(s_dve, P.cur(s_dve))], inc=s_dve)
        bna = P.cur(s_dve)
        P.op("scalar", lambda s, q0=q0: s.activation(bnst[:, q0 + 14:q0 + 15], bnst[:, q0 + 13:q0 + 14],
                                                     AF.Ln, bias=epsP[:]),
             waits=[(s_dve, bna)], inc=s_act)
        P.op("scalar", lambda s, q0=q0: s.activation(bnst[:, q0 + 15:q0 + 16], bnst[:, q0 + 14:q0 + 15],
                                                     AF.Exp, scale=-0.5),
             waits=[(s_act, P.cur(s_act))], inc=s_act)
        P.op("vector", lambda v, xt=xt, q0=q0: v.tensor_scalar(ocen[:], xt[:], bnst[:, q0 + 12:q0 + 13],
                                                               bnst[:, q0 + 15:q0 + 16],
                                                               ALU.subtract, ALU.mult),
             waits=[(s_act, P.cur(s_act))], inc=s_dve)
        sc = P.cur(s_dve)
        for f in range(NF):
            gi = tt * NF + f
            bank = (ps_bc, ps_oa)[gi % 2]
            w = [(s_dve, sc)]
            if gi >= 2:
                w.append((s_dve, oxp[gi - 2]))
            P.op("tensor", lambda t, f=f, bank=bank: t.transpose(
                bank[:, 0:128], ocen[:, 128 * f:128 * (f + 1)], ident[:]), waits=w, inc=s_pe)
            tpn = P.cur(s_pe)
            P.op("vector", lambda v, f=f, bank=bank: v.tensor_copy(
                xn32[:, 128 * f:128 * (f + 1)], bank[:, 0:128]),
                 waits=[(s_pe, tpn)], inc=s_dve)
            P.op("vector", lambda v, f=f, tt=tt, bank=bank: v.tensor_copy(
                xn_own[:, TO * f + 128 * tt:TO * f + 128 * (tt + 1)], bank[:, 0:128]),
                 inc=s_dve)
            oxp[gi] = P.cur(s_dve)
        xn32_done = P.cur(s_dve)
        w = [(s_dw, rw_ld), (s_dve, xn32_done)]
        if tt >= 1:
            w.append((s_dve, oln[tt - 1]))
        def rmm(t):
            last = None
            for f in range(NF):
                last = t.matmul(ps_x[:, 0:4], xn32[:, 128 * f:128 * (f + 1)],
                                rw32[:, 4 * f:4 * (f + 1)],
                                start=(f == 0), stop=(f == NF - 1))
            return last
        P.op("tensor", rmm, waits=w, inc=s_pe)
        rm = P.cur(s_pe)
        P.op("vector", lambda v: v.tensor_tensor(r_tiny[:, 0:4], ps_x[:, 0:4], rbc[:], ALU.add),
             waits=[(s_pe, rm)], inc=s_dve)
        if debug:
            P.op("vector", lambda v, tt=tt: v.tensor_copy(dbg_lg_sb[:, 4 * tt:4 * (tt + 1)],
                                                          r_tiny[:, 0:4]),
                 waits=[(s_dve, P.cur(s_dve))], inc=s_dve)
        P.op("vector", lambda v: v.tensor_reduce(r_tiny[:, 4:5], r_tiny[:, 0:4], AX.X, ALU.max),
             waits=[(s_dve, P.cur(s_dve))], inc=s_dve)
        P.op("vector", lambda v: v.tensor_tensor(r_tiny[:, 5:6], r_tiny[:, 0:1], r_tiny[:, 1:2],
                                                 ALU.max), waits=[(s_dve, P.cur(s_dve))], inc=s_dve)
        P.op("vector", lambda v: v.tensor_tensor(r_tiny[:, 6:7], r_tiny[:, 0:1], r_tiny[:, 1:2],
                                                 ALU.min), waits=[(s_dve, P.cur(s_dve))], inc=s_dve)
        P.op("vector", lambda v: v.tensor_tensor(bc_sb[:, 0:1], r_tiny[:, 2:3], r_tiny[:, 3:4],
                                                 ALU.max), waits=[(s_dve, P.cur(s_dve))], inc=s_dve)
        P.op("vector", lambda v: v.tensor_tensor(bc_sb[:, 1:2], r_tiny[:, 2:3], r_tiny[:, 3:4],
                                                 ALU.min), waits=[(s_dve, P.cur(s_dve))], inc=s_dve)
        P.op("vector", lambda v: v.tensor_tensor(bc_sb[:, 2:3], r_tiny[:, 5:6], bc_sb[:, 0:1],
                                                 ALU.min), waits=[(s_dve, P.cur(s_dve))], inc=s_dve)
        P.op("vector", lambda v: v.tensor_tensor(bc_sb[:, 3:4], r_tiny[:, 6:7], bc_sb[:, 1:2],
                                                 ALU.max), waits=[(s_dve, P.cur(s_dve))], inc=s_dve)
        P.op("vector", lambda v: v.tensor_tensor(bc_sb[:, 4:5], bc_sb[:, 2:3], bc_sb[:, 3:4],
                                                 ALU.max), waits=[(s_dve, P.cur(s_dve))], inc=s_dve)
        P.op("vector", lambda v: v.tensor_scalar(r_tiny[:, 7:8], r_tiny[:, 4:5], -1.0, None,
                                                 ALU.mult), waits=[(s_dve, P.cur(s_dve))], inc=s_dve)
        pre_exp = P.cur(s_dve)
        P.op("scalar", lambda s: s.activation(bc_sb[:, 8:12], r_tiny[:, 0:4], AF.Exp,
                                              bias=r_tiny[:, 7:8]),
             waits=[(s_dve, pre_exp)], inc=s_act)
        P.op("vector", lambda v: v.tensor_scalar(bc_sb[:, 12:16], r_tiny[:, 0:4], bc_sb[:, 4:5],
                                                 1.0, ALU.is_ge, ALU.mult),
             waits=[(s_act, P.cur(s_act))], inc=s_dve)
        P.op("vector", lambda v: v.tensor_tensor(bc_sb[:, 16:20], bc_sb[:, 8:12], bc_sb[:, 12:16],
                                                 ALU.mult), waits=[(s_dve, P.cur(s_dve))], inc=s_dve)
        P.op("vector", lambda v: v.tensor_reduce(bc_sb[:, 20:21], bc_sb[:, 16:20], AX.X, ALU.add),
             waits=[(s_dve, P.cur(s_dve))], inc=s_dve)
        P.op("vector", lambda v: v.reciprocal(bc_sb[:, 21:22], bc_sb[:, 20:21]),
             waits=[(s_dve, P.cur(s_dve))], inc=s_dve)
        P.op("vector", lambda v, tt=tt: v.tensor_scalar(w4[:, 4 * tt:4 * (tt + 1)],
                                                        bc_sb[:, 16:20], bc_sb[:, 21:22], 0.0,
                                                        ALU.mult, ALU.add),
             waits=[(s_dve, P.cur(s_dve))], inc=s_dve)
        oln[tt] = P.cur(s_dve)
    xn_own_done = P.cur(s_dve)
    router_done = P.cur(s_dve)
    if debug:
        P.op("gpsimd", lambda g: g.dma_start(out=dbg["dbg_w4"], in_=w4[:]),
             waits=[(s_dve, router_done)], inc=s_dw, inc_k=16)
        P.op("gpsimd", lambda g: g.dma_start(out=dbg["dbg_lg"], in_=dbg_lg_sb[:]),
             waits=[(s_dve, router_done)], inc=s_dw, inc_k=16)
        P.op("gpsimd", lambda g: g.dma_start(out=dbg["dbg_xnown"], in_=xn_own[:]),
             waits=[(s_dve, xn_own_done)], inc=s_dw, inc_k=16)

    slot_free = {wslot[0].name: [(s_dve, xn_own_done)],
                 wslot[1].name: [(s_dve, xn_own_done)],
                 wslot[2].name: [(s_pe, P.cur(s_pe))]}
    wslot3 = sb("wslot3", [128, 4096], BF16, at=R4)
    wslot4 = sb("wslot4", [128, 4096], BF16, at=R4 + 8192)
    ring_slots = [wslot[0], wslot[1], wslot[2], wslot3, wslot4]
    slot_free[wslot3.name] = [(s_dx, P.cur(s_dx))]
    slot_free[wslot4.name] = [(s_av, P.cur(s_av))]
    RING = [0]
    ST_EVAC = {0: None, 1: None}

    def ring_slot():
        s = ring_slots[RING[0] % 5]
        RING[0] += 1
        return s
    eout_guard = {0: 0, 1: 0}

    def load_w_cols(wname, c0, c1, nrows, slot):
        nch = nrows // 128
        ncol = c1 - c0
        P.op("gpsimd", lambda g: g.dma_start(
            out=slot[:, 0:nch * ncol].rearrange("p (a j) -> p a j", a=nch),
            in_=dT[wname][:, c0:c1].rearrange("(a p) j -> p a j", p=128)),
             waits=slot_free[slot.name], inc=s_dw, inc_k=16)
        return P.cur(s_dw)

    def load_w_rows(wname, r0, r1, slot):
        nch = (r1 - r0) // 128
        P.op("gpsimd", lambda g: g.dma_start(
            out=slot[:, 0:nch * 1024].rearrange("p (a n) -> p a n", a=nch),
            in_=dT[wname][r0:r1, :].rearrange("(a p) n -> p a n", p=128)),
             waits=slot_free[slot.name], inc=s_dw, inc_k=16)
        return P.cur(s_dw)

    def fm_stage(src_get, nK, wname, wcols, dst_put, bias_name, act, extra_get=None,
                 src_ready=None):
        nrows = nK * 128
        pw = 4096 // nK * 1
        pw = min(512, (4096 // nK) // 128 * 128)
        assert pw >= 128
        npieces = wcols // pw
        if src_ready is None:
            src_ready = [(s_act, P.cur(s_act)), (s_dve, P.cur(s_dve))]
        for piece in range(npieces):
            slot = ring_slot()
            wdone = load_w_cols(wname, pw * piece, pw * (piece + 1), nrows, slot)
            for j in range(pw // 128):
                fo = piece * (pw // 128) + j
                bank = (ps_st0, ps_st1)[fo % 2]
                bsl = bank[:, 0:512]
                w = [(s_dw, wdone)] + src_ready
                if ST_EVAC[fo % 2] is not None:
                    w.append(ST_EVAC[fo % 2])
                def mm(t, slot=slot, j=j, bsl=bsl, pw=pw, nK=nK, src_get=src_get):
                    last = None
                    for f in range(nK):
                        last = t.matmul(bsl, slot[:, pw * f + 128 * j:pw * f + 128 * (j + 1)],
                                        src_get(f), start=(f == 0), stop=(f == nK - 1))
                    return last
                P.op("tensor", mm, waits=w, inc=s_pe)
                slot_free[slot.name] = [(s_pe, P.cur(s_pe))]
                mmn = P.cur(s_pe)
                if extra_get is not None:
                    def ev(v, fo=fo, bsl=bsl, dst_put=dst_put, bias_name=bias_name,
                           extra_get=extra_get):
                        return v.scalar_tensor_tensor(dst_put(fo), bsl, bias_col(bias_name, fo),
                                                      extra_get(fo), ALU.add, ALU.add)
                    P.op("vector", ev, waits=[(s_pe, mmn)], inc=s_dve)
                    ST_EVAC[fo % 2] = (s_dve, P.cur(s_dve))
                else:
                    def ev(s, fo=fo, bsl=bsl, dst_put=dst_put, bias_name=bias_name, act=act):
                        return s.activation(dst_put(fo), bsl, act, bias=bias_col(bias_name, fo))
                    P.op("scalar", ev, waits=[(s_pe, mmn)], inc=s_act)
                    ST_EVAC[fo % 2] = (s_act, P.cur(s_act))

    def tm_partial(src_get, f_list, wname, r0, brow_off, e_idx, init):
        nch = len(f_list)
        slot = wslot[2]
        wdone = load_w_rows(wname, r0, r0 + 128 * nch, slot)
        for gt in range(16):
            for half in range(2):
                bi = (gt * 2 + half) % 2
                bank = (ps_oa, ps_ob)[bi]
                w = [(s_dw, wdone), (s_dve, eout_guard[bi])]
                def mm(t, gt=gt, half=half, bank=bank, f_list=f_list, src_get=src_get,
                       brow_off=brow_off, nch=nch):
                    last = None
                    for i, f in enumerate(f_list):
                        is_last = (i == nch - 1) and (brow_off is None)
                        last = t.matmul(bank[:, 0:512], src_get(f, gt),
                                        slot[:, 1024 * i + 512 * half:1024 * i + 512 * (half + 1)],
                                        start=(i == 0), stop=is_last)
                    if brow_off is not None:
                        last = t.matmul(bank[:, 0:512], ones_r128_b[0:1, :],
                                        brow_bf[0:1, brow_off + 512 * half:brow_off + 512 * (half + 1)],
                                        start=False, stop=True)
                    return last
                P.op("tensor", mm, waits=w, inc=s_pe)
                mmn = P.cur(s_pe)
                def acc(v, gt=gt, half=half, bank=bank, e_idx=e_idx, init=init):
                    dst = out_acc[:, D * gt + 512 * half:D * gt + 512 * (half + 1)]
                    wc = w4[:, 4 * gt + e_idx:4 * gt + e_idx + 1]
                    if init:
                        return v.tensor_scalar(dst, bank[:, 0:512], wc, 0.0, ALU.mult, ALU.add)
                    return v.scalar_tensor_tensor(dst, bank[:, 0:512], wc, dst, ALU.mult, ALU.add)
                P.op("vector", acc, waits=[(s_pe, mmn)], inc=s_dve)
                eout_guard[bi] = P.cur(s_dve)
        slot_free[slot.name] = [(s_pe, P.cur(s_pe))]

    def ch_get(buf):
        return lambda f: buf[:, 512 * f:512 * (f + 1)]

    def ch_get2(bufs):
        return lambda f: bufs[f // 8][:, 512 * (f % 8):512 * (f % 8 + 1)]

    def ch_put(buf):
        return lambda fo: buf[:, 512 * fo:512 * (fo + 1)]

    def ch_put2(bufs):
        return lambda fo: bufs[fo // 8][:, 512 * (fo % 8):512 * (fo % 8 + 1)]

    def xn_get(tblk):
        return lambda f: xn_own[:, TO * f + 512 * tblk:TO * f + 512 * (tblk + 1)]

    def ln_fm(src_buf, g_name, b_name, dst_buf):
        P.op("vector", lambda v: v.tensor_tensor(chD[:], src_buf[:], src_buf[:], ALU.mult),
             waits=[(s_act, P.cur(s_act)), (s_dve, P.cur(s_dve))], inc=s_dve)
        sqd = P.cur(s_dve)
        def stats(t):
            last = None
            for f in range(NF):
                t.matmul(ps_x[0:1, 0:512], ones_col_bf[:], src_buf[:, 512 * f:512 * (f + 1)],
                         start=(f == 0), stop=(f == NF - 1))
            for f in range(NF):
                last = t.matmul(ps_bc[0:1, 0:512], ones_col_bf[:], chD[:, 512 * f:512 * (f + 1)],
                                start=(f == 0), stop=(f == NF - 1))
            return last
        P.op("tensor", stats, waits=[(s_dve, sqd)], inc=s_pe)
        std = P.cur(s_pe)
        P.op("vector", lambda v: v.tensor_scalar(stat_sb[0:1, 0:512], ps_x[0:1, 0:512], 1.0 / D,
                                                 None, ALU.mult), waits=[(s_pe, std)], inc=s_dve)
        P.op("vector", lambda v: v.tensor_tensor(stat_sb[0:1, 512:1024], stat_sb[0:1, 0:512],
                                                 stat_sb[0:1, 0:512], ALU.mult),
             waits=[(s_dve, P.cur(s_dve))], inc=s_dve)
        P.op("vector", lambda v: v.tensor_scalar(bc_sb[0:1, 0:512], ps_bc[0:1, 0:512], 1.0 / D,
                                                 None, ALU.mult), waits=[(s_dve, P.cur(s_dve))], inc=s_dve)
        P.op("vector", lambda v: v.tensor_tensor(bc_sb[0:1, 0:512], bc_sb[0:1, 0:512],
                                                 stat_sb[0:1, 512:1024], ALU.subtract),
             waits=[(s_dve, P.cur(s_dve))], inc=s_dve)
        vd = P.cur(s_dve)
        P.op("scalar", lambda s: s.activation(bc_sb[0:1, 0:512], bc_sb[0:1, 0:512], AF.Ln,
                                              bias=eps_t[0:1, 0:1]), waits=[(s_dve, vd)], inc=s_act)
        P.op("scalar", lambda s: s.activation(stat_sb[0:1, 512:1024], bc_sb[0:1, 0:512], AF.Exp,
                                              scale=-0.5), waits=[(s_act, P.cur(s_act))], inc=s_act)
        rd = P.cur(s_act)
        P.op("tensor", lambda t: t.matmul(ps_x[:, 0:512], ones_r128_f[0:1, :], stat_sb[0:1, 0:512],
                                          start=True, stop=True), waits=[(s_act, rd)], inc=s_pe)
        P.op("tensor", lambda t: t.matmul(ps_bc[:, 0:512], ones_r128_f[0:1, :],
                                          stat_sb[0:1, 512:1024], start=True, stop=True), inc=s_pe)
        bcd = P.cur(s_pe)
        for f in range(NF):
            P.op("vector", lambda v, f=f: v.tensor_tensor(chD[:, 512 * f:512 * (f + 1)],
                                                          src_buf[:, 512 * f:512 * (f + 1)],
                                                          ps_x[:, 0:512], ALU.subtract),
                 waits=[(s_pe, bcd)], inc=s_dve)
            P.op("vector", lambda v, f=f: v.tensor_tensor(chD[:, 512 * f:512 * (f + 1)],
                                                          chD[:, 512 * f:512 * (f + 1)],
                                                          ps_bc[:, 0:512], ALU.mult),
                 waits=[(s_dve, P.cur(s_dve))], inc=s_dve)
            P.op("vector", lambda v, f=f: v.tensor_scalar(dst_buf[:, 512 * f:512 * (f + 1)],
                                                          chD[:, 512 * f:512 * (f + 1)],
                                                          bias_col(g_name, f), bias_col(b_name, f),
                                                          ALU.mult, ALU.add),
                 waits=[(s_dve, P.cur(s_dve))], inc=s_dve)

    for tblk in range(4):
        fm_stage(xn_get(tblk), NF, "ce_syn_w", D, ch_put(chA), "ce_syn_b", AF.Identity,
                 src_ready=[(s_dve, xn_own_done)] if tblk > 0 else None)
        fm_stage(ch_get(chA), NF, "ce_wv", D, ch_put(chB), "ce_bv", AF.Identity)
        fm_stage(ch_get(chB), NF, "ce_wo", D, ch_put(chC), "ce_bo", None, extra_get=ch_get(chA))
        ln_fm(chC, "ce_n1g", "ce_n1b", chA)
        fm_stage(ch_get(chA), NF, "ce_f1w", 2 * D, ch_put2((chC, chD)), "ce_f1b", AF.Relu)
        fm_stage(ch_get2((chC, chD)), 16, "ce_f2w", D, ch_put(chB), "ce_f2b", None,
                 extra_get=ch_get(chA))
        ln_fm(chB, "ce_n2g", "ce_n2b", chC)
        def srcC(f, gt, tblk=tblk):
            ti = gt - 4 * tblk
            return chC[:, 512 * f + 128 * ti:512 * f + 128 * (ti + 1)]
        def tm_partial_blk(src_get, f_list, wname, r0, brow_off, e_idx, init, tblk=tblk,
                           src_ready=None):
            nch = len(f_list)
            slot = ring_slot()
            wdone = load_w_rows(wname, r0, r0 + 128 * nch, slot)
            if src_ready is None:
                src_ready = [(s_act, P.cur(s_act)), (s_dve, P.cur(s_dve))]
            for ti in range(4):
                gt = 4 * tblk + ti
                for half in range(2):
                    bi = (gt * 2 + half) % 2
                    bank = (ps_oa, ps_ob)[bi]
                    w = [(s_dw, wdone), (s_dve, eout_guard[bi])] + src_ready
                    def mm(t, gt=gt, half=half, bank=bank, f_list=f_list, src_get=src_get,
                           brow_off=brow_off, nch=nch, slot=slot):
                        last = None
                        for i, f in enumerate(f_list):
                            is_last = (i == nch - 1) and (brow_off is None)
                            last = t.matmul(bank[:, 0:512], src_get(f, gt),
                                            slot[:, 1024 * i + 512 * half:1024 * i + 512 * (half + 1)],
                                            start=(i == 0), stop=is_last)
                        if brow_off is not None:
                            last = t.matmul(bank[:, 0:512], ones_r128_b[0:1, :],
                                            brow_bf[0:1, brow_off + 512 * half:
                                                    brow_off + 512 * (half + 1)],
                                            start=False, stop=True)
                        return last
                    P.op("tensor", mm, waits=w, inc=s_pe)
                    mmn = P.cur(s_pe)
                    def acc(v, gt=gt, half=half, bank=bank, e_idx=e_idx, init=init):
                        dst = out_acc[:, D * gt + 512 * half:D * gt + 512 * (half + 1)]
                        wc = w4[:, 4 * gt + e_idx:4 * gt + e_idx + 1]
                        if init:
                            return v.tensor_scalar(dst, bank[:, 0:512], wc, 0.0, ALU.mult, ALU.add)
                        return v.scalar_tensor_tensor(dst, bank[:, 0:512], wc, dst, ALU.mult,
                                                      ALU.add)
                    P.op("vector", acc, waits=[(s_pe, mmn)], inc=s_dve)
                    eout_guard[bi] = P.cur(s_dve)
            slot_free[slot.name] = [(s_pe, P.cur(s_pe))]
        tm_partial_blk(srcC, [0, 1, 2, 3], "ce_gen_w", 0, None, 2, True)
        tm_partial_blk(srcC, [4, 5, 6, 7], "ce_gen_w", 512, D, 2, False)

    def e0_e3(expert):
        w1n, w2n = ("sw_w1", "sw_w2") if expert == 0 else ("ml_w1", "ml_w2")
        for sblk in range(8):
            s1 = wslot[0]
            w1d = load_w_cols(w1n, 512 * sblk, 512 * (sblk + 1), D, s1)
            if expert == 0:
                s3 = wslot[1]
                w3d = load_w_cols("sw_w3", 512 * sblk, 512 * (sblk + 1), D, s3)
            s2 = wslot[2]
            w2d = load_w_rows(w2n, 512 * sblk, 512 * (sblk + 1), s2)
            start_ready = [(s_dve, xn_own_done)] if sblk > 0 else \
                [(s_act, P.cur(s_act)), (s_dve, P.cur(s_dve))]
            for ic in range(4):
                for tq in range(4):
                    it = ic * 4 + tq
                    bank = (ps_st0, ps_st1)[it % 2]
                    w = [(s_dw, w1d)] + start_ready
                    if ST_EVAC[it % 2] is not None:
                        w.append(ST_EVAC[it % 2])
                    def amm(t, s1=s1, ic=ic, tq=tq, bank=bank):
                        last = None
                        for f in range(NF):
                            last = t.matmul(bank[:, 0:512],
                                            s1[:, 512 * f + 128 * ic:512 * f + 128 * (ic + 1)],
                                            xn_own[:, TO * f + 512 * tq:TO * f + 512 * (tq + 1)],
                                            start=(f == 0), stop=(f == NF - 1))
                        return last
                    P.op("tensor", amm, waits=w, inc=s_pe)
                    amn = P.cur(s_pe)
                    if expert == 0:
                        P.op("scalar", lambda s, bank=bank: s.activation(asil[:], bank[:, 0:512],
                                                                         AF.Silu),
                             waits=[(s_pe, amn), (s_dve, P.cur(s_dve))], inc=s_act)
                        sil = P.cur(s_act)
                        def bmm(t, s3=s3, ic=ic, tq=tq, bank=bank):
                            last = None
                            for f in range(NF):
                                last = t.matmul(bank[:, 512:1024],
                                                s3[:, 512 * f + 128 * ic:512 * f + 128 * (ic + 1)],
                                                xn_own[:, TO * f + 512 * tq:TO * f + 512 * (tq + 1)],
                                                start=(f == 0), stop=(f == NF - 1))
                            return last
                        wb = [(s_dw, w3d)]
                        if ST_EVAC[it % 2] is not None:
                            wb.append(ST_EVAC[it % 2])
                        P.op("tensor", bmm, waits=wb, inc=s_pe)
                        bmn = P.cur(s_pe)
                        def gmul(v, ic=ic, tq=tq, bank=bank):
                            return v.tensor_tensor(
                                g_e0[:, 2048 * ic + 512 * tq:2048 * ic + 512 * (tq + 1)],
                                bank[:, 512:1024], asil[:], ALU.mult)
                        P.op("vector", gmul, waits=[(s_pe, bmn), (s_act, sil)], inc=s_dve)
                        ST_EVAC[it % 2] = (s_dve, P.cur(s_dve))
                    else:
                        def gact(s, ic=ic, tq=tq, bank=bank, sblk=sblk):
                            return s.activation(
                                g_e0[:, 2048 * ic + 512 * tq:2048 * ic + 512 * (tq + 1)],
                                bank[:, 0:512], AF.Gelu, bias=bias_col("ml_b1", 4 * sblk + ic))
                        P.op("scalar", gact, waits=[(s_pe, amn)], inc=s_act)
                        ST_EVAC[it % 2] = (s_act, P.cur(s_act))
            slot_free[s1.name] = [(s_pe, P.cur(s_pe))]
            if expert == 0:
                slot_free[s3.name] = [(s_pe, P.cur(s_pe))]
            g_ready = (s_dve, P.cur(s_dve)) if expert == 0 else (s_act, P.cur(s_act))
            for gt in range(16):
                ti = gt % 4
                for half in range(2):
                    bi = (gt * 2 + half) % 2
                    bank = (ps_oa, ps_ob)[bi]
                    bias_here = (expert == 3 and sblk == 7)
                    w = [(s_dw, w2d), (s_dve, eout_guard[bi]), g_ready]
                    def mm(t, gt=gt, half=half, bank=bank, bias_here=bias_here, s2=s2):
                        last = None
                        for ic in range(4):
                            last = t.matmul(bank[:, 0:512],
                                            g_e0[:, 2048 * ic + 128 * gt:2048 * ic + 128 * (gt + 1)],
                                            s2[:, 1024 * ic + 512 * half:1024 * ic + 512 * (half + 1)],
                                            start=(ic == 0), stop=(ic == 3 and not bias_here))
                        if bias_here:
                            last = t.matmul(bank[:, 0:512], ones_r128_b[0:1, :],
                                            brow_bf[0:1, 2 * D + 512 * half:2 * D + 512 * (half + 1)],
                                            start=False, stop=True)
                        return last
                    P.op("tensor", mm, waits=w, inc=s_pe)
                    mmn = P.cur(s_pe)
                    e_idx = 0 if expert == 0 else 3
                    def acc(v, gt=gt, half=half, bank=bank, e_idx=e_idx):
                        dst = out_acc[:, D * gt + 512 * half:D * gt + 512 * (half + 1)]
                        wc = w4[:, 4 * gt + e_idx:4 * gt + e_idx + 1]
                        return v.scalar_tensor_tensor(dst, bank[:, 0:512], wc, dst, ALU.mult,
                                                      ALU.add)
                    P.op("vector", acc, waits=[(s_pe, mmn)], inc=s_dve)
                    eout_guard[bi] = P.cur(s_dve)
            slot_free[s2.name] = [(s_pe, P.cur(s_pe))]

    e0_e3(0)
    e0_e3(3)

    for tblk in range(4):
        fm_stage(xn_get(tblk), NF, "me_eq_w", D, ch_put(chA), "me_eq_b", AF.Identity,
                 src_ready=[(s_dve, xn_own_done)])
        fm_stage(ch_get(chA), NF, "me_wv", D, ch_put(chB), "me_bv", AF.Identity)
        fm_stage(ch_get(chB), NF, "me_wo", D, ch_put(chA), "me_bo", AF.Identity)
        fm_stage(ch_get(chA), NF, "me_c1w", 2 * D, ch_put2((chC, chD)), "me_c1b", AF.Gelu)
        def srcCD(f, gt, tblk=tblk):
            ti = gt - 4 * tblk
            buf = (chC, chD)[f // 8]
            fr = f % 8
            return buf[:, 512 * fr + 128 * ti:512 * fr + 128 * (ti + 1)]
        def tm_partial_blk1(f_list, r0, brow_off, init, tblk=tblk):
            nch = len(f_list)
            slot = ring_slot()
            wdone = load_w_rows("me_c2w", r0, r0 + 128 * nch, slot)
            src_ready = [(s_act, P.cur(s_act)), (s_dve, P.cur(s_dve))]
            for ti in range(4):
                gt = 4 * tblk + ti
                for half in range(2):
                    bi = (gt * 2 + half) % 2
                    bank = (ps_oa, ps_ob)[bi]
                    w = [(s_dw, wdone), (s_dve, eout_guard[bi])] + src_ready
                    def mm(t, gt=gt, half=half, bank=bank, f_list=f_list, brow_off=brow_off,
                           nch=nch, slot=slot, srcCD=srcCD):
                        last = None
                        for i, f in enumerate(f_list):
                            is_last = (i == nch - 1) and (brow_off is None)
                            last = t.matmul(bank[:, 0:512], srcCD(f, gt),
                                            slot[:, 1024 * i + 512 * half:1024 * i + 512 * (half + 1)],
                                            start=(i == 0), stop=is_last)
                        if brow_off is not None:
                            last = t.matmul(bank[:, 0:512], ones_r128_b[0:1, :],
                                            brow_bf[0:1, brow_off + 512 * half:
                                                    brow_off + 512 * (half + 1)],
                                            start=False, stop=True)
                        return last
                    P.op("tensor", mm, waits=w, inc=s_pe)
                    mmn = P.cur(s_pe)
                    def acc(v, gt=gt, half=half, bank=bank):
                        dst = out_acc[:, D * gt + 512 * half:D * gt + 512 * (half + 1)]
                        wc = w4[:, 4 * gt + 1:4 * gt + 2]
                        return v.scalar_tensor_tensor(dst, bank[:, 0:512], wc, dst, ALU.mult,
                                                      ALU.add)
                    P.op("vector", acc, waits=[(s_pe, mmn)], inc=s_dve)
                    eout_guard[bi] = P.cur(s_dve)
            slot_free[slot.name] = [(s_pe, P.cur(s_pe))]
        for qi in range(4):
            tm_partial_blk1(list(range(4 * qi, 4 * qi + 4)), 512 * qi,
                            (0 if qi == 3 else None), False)
    experts_done_dve = P.cur(s_dve)

    if debug:
        P.op("gpsimd", lambda g: g.dma_start(out=dbg["dbg_acc"], in_=out_acc[:]),
             waits=[(s_dve, experts_done_dve)], inc=s_dw, inc_k=16)

    fin_dma = {}
    fsem = [s_xa, s_xb, s_xc, s_xd]
    fsem_base = [P.cur(s) for s in fsem]
    for gt in range(16):
        i = gt % 4
        fa, fx = fin_rs[i], fin_x[i]
        sx = fsem[i]
        w = [(s_cc, 1), (s_dve, experts_done_dve)]
        if gt >= 4:
            w.append((s_do, fin_dma[gt - 4]))
        P.op("sync", lambda sy, gt=gt, fa=fa: sy.dma_start(
            out=fa[:], in_=rs_out[128 * gt:128 * (gt + 1), :]), waits=w, inc=sx, inc_k=16)
        P.op("sync", lambda sy, gt=gt, fx=fx: sy.dma_start(
            out=fx[:], in_=dT["x_own"][128 * gt:128 * (gt + 1), :]), inc=sx, inc_k=16)
        dmad = P.cur(sx)
        P.op("vector", lambda v, fa=fa: v.tensor_tensor(fa[:], fa[:], bp_bc[:], ALU.add),
             waits=[(sx, dmad)], inc=s_dve)
        P.op("vector", lambda v, fa=fa, fx=fx: v.tensor_tensor(fa[:], fa[:], fx[:], ALU.add),
             waits=[(s_dve, P.cur(s_dve))], inc=s_dve)
        P.op("vector", lambda v, fa=fa, gt=gt: v.tensor_tensor(fa[:], fa[:],
                                                               out_acc[:, D * gt:D * (gt + 1)],
                                                               ALU.add),
             waits=[(s_dve, P.cur(s_dve))], inc=s_dve)
        addd = P.cur(s_dve)
        P.op("sync", lambda sy, gt=gt, fa=fa: sy.dma_start(
            out=out[128 * gt:128 * (gt + 1), :], in_=fa[:]),
             waits=[(s_dve, addd)], inc=s_do, inc_k=16)
        fin_dma[gt] = P.cur(s_do)
    total_do = P.cur(s_do)
    P.op("sync", lambda sy: sy.nop(), waits=[(s_do, total_do)])

    import contextlib
    with contextlib.ExitStack() as st:
        handles = {}
        for name in list(P.counts):
            handles[name] = st.enter_context(nc.semaphore(name))
        block = st.enter_context(nc.Block())
        P.emit(block, handles)
    return nc


_NC_CACHE = {}


def _get_nc(debug=False):
    key = bool(debug)
    if key not in _NC_CACHE:
        _NC_CACHE[key] = build_nc(debug=key)
    return _NC_CACHE[key]


def make_in_maps(inputs):
    x = np.ascontiguousarray(np.asarray(inputs["L0_attn_x"], dtype=np.float32))
    B = x.shape[0]
    g = lambda n: np.ascontiguousarray(np.asarray(inputs[n], dtype=np.float32))
    Wq, Wk, Wv, Wp, bp = (g("L0_attn_Wq"), g("L0_attn_Wk"), g("L0_attn_Wv"),
                          g("L0_attn_Wp"), g("L0_attn_bp"))
    F = {k[7:]: g(k) for k in inputs if k.startswith("L0_ffn_")}

    bias_fm = np.zeros((128, BIAS_FM_COLS), dtype=np.float32)
    for name, nch in BIAS_FM_SPECS:
        arr = F[{"me_eq_b": "me_eq_b", "me_bv": "me_bv", "me_bo": "me_bo", "me_c1b": "me_c1b",
                 "ce_syn_b": "ce_syn_b", "ce_bv": "ce_bv", "ce_bo": "ce_bo", "ce_n1g": "ce_n1g",
                 "ce_n1b": "ce_n1b", "ce_f1b": "ce_f1b", "ce_f2b": "ce_f2b", "ce_n2g": "ce_n2g",
                 "ce_n2b": "ce_n2b", "ml_b1": "ml_b1"}[name]]
        for c in range(nch):
            bias_fm[:, BIAS_FM_OFF[name] + c] = arr[128 * c:128 * (c + 1)]
    brow_bf = np.concatenate([F["me_c2b"], F["ce_gen_b"], F["ml_b2"]])[None, :].astype(np.float32)
    brow_f = np.concatenate([bp, F["router_b"], F["load_balancer"]])[None, :].astype(np.float32)

    common = dict(
        router_w=F["router_w"], sw_w1=F["sw_w1"], sw_w3=F["sw_w3"], sw_w2=F["sw_w2"],
        me_eq_w=F["me_eq_w"], me_wv=F["me_wv"], me_wo=F["me_wo"], me_c1w=F["me_c1w"],
        me_c2w=F["me_c2w"], ce_syn_w=F["ce_syn_w"], ce_wv=F["ce_wv"], ce_wo=F["ce_wo"],
        ce_f1w=F["ce_f1w"], ce_f2w=F["ce_f2w"], ce_gen_w=F["ce_gen_w"],
        ml_w1=F["ml_w1"], ml_w2=F["ml_w2"],
        bias_fm=bias_fm, bias_rows_bf=brow_bf, bias_rows_f32=brow_f,
    )
    in_maps = []
    for c in range(8):
        b, pos = c // 2, c % 2
        m = dict(common)
        m["x_full"] = x[b]
        m["x_own"] = np.ascontiguousarray(x[b, TO * pos:TO * (pos + 1)])
        m["wq"] = np.ascontiguousarray(Wq[:, 512 * pos:512 * (pos + 1)])
        m["wk"] = np.ascontiguousarray(Wk[:, 512 * pos:512 * (pos + 1)])
        m["wv"] = np.ascontiguousarray(Wv[:, 512 * pos:512 * (pos + 1)])
        m["wp"] = np.ascontiguousarray(Wp[512 * pos:512 * (pos + 1), :])
        in_maps.append(m)
    return in_maps, B


def kernel(**inputs):
    nc = _get_nc(debug=False)
    in_maps, B = make_in_maps(inputs)
    res = run_bass_kernel_spmd(nc, in_maps, core_ids=list(range(8)))
    outp = np.empty((B, T, D), dtype=np.float32)
    for c in range(8):
        b, pos = c // 2, c % 2
        outp[b, TO * pos:TO * (pos + 1)] = res.results[c]["out"]
    return outp
